# version 27
# baseline (speedup 1.0000x reference)
import contextlib
import numpy as np
import concourse.bass as bass
import concourse.mybir as mybir
from concourse.bass import AP
from concourse.bass_utils import run_bass_kernel_spmd
from concourse.alu_op_type import AluOpType as OP

F32 = mybir.dt.float32
BF16 = mybir.dt.bfloat16
AF = mybir.ActivationFunctionType

D4 = 4096
NCH = 32
TT = 512
NCORES = 8
EPS = 1e-5
LOG2 = 0.6931471805599453


class Sync:

    def __init__(self, nc, st):
        self.nc = nc
        self.st = st
        self.eng = {"T": nc.tensor, "V": nc.vector, "S": nc.scalar,
                    "G": nc.gpsimd, "Y": nc.sync}
        self.sem = {k: st.enter_context(nc.semaphore(f"sem{k}")) for k in self.eng}
        self.cnt = {k: 0 for k in self.eng}
        self.last = {}

    def new_sem(self, key):
        self.sem[key] = self.st.enter_context(self.nc.semaphore(f"sem{key}"))
        self.cnt[key] = 0

    def inc(self, k, instr, n=1):
        instr.then_inc(self.sem[k], n)
        self.cnt[k] += n
        return (k, self.cnt[k])

    def tok(self, k):
        return (k, self.cnt[k])

    def wait(self, consumer, tok):
        if tok is None:
            return
        k, v = tok
        if v <= 0:
            return
        key = (consumer, k)
        if self.last.get(key, 0) >= v:
            return
        self.eng[consumer].wait_ge(self.sem[k], v)
        self.last[key] = v

    def waits(self, consumer, toks):
        best = {}
        for t in toks:
            if t is None:
                continue
            k, v = t
            if v > best.get(k, 0):
                best[k] = v
        for k, v in best.items():
            self.wait(consumer, (k, v))


def build_nc(ntile):
    ntok = ntile * TT
    nc = bass.Bass("TRN2", target_bir_lowering=False)

    def dti(name, shp, dtype=F32, kind="ExternalInput"):
        return nc.dram_tensor(name, shp, dtype, kind=kind)

    uT = dti("uT", [D4, ntok])
    WT = dti("WT", [D4, D4])
    CatA = dti("CatA", [D4, 36])
    BmTd = dti("BmTd", [32, D4])
    biasd = dti("biasd", [D4])
    smd = {n: dti(n, [128]) for n in ["lam_re", "lam_im", "p_re", "p_im", "q_re",
                                      "q_im", "ct_re", "ct_im", "Bpv"]}
    vctre_d = dti("VcT_re", [64, 64]); vctim_d = dti("VcT_im", [64, 64])
    Dv = dti("Dv", [1]); lsd = dti("lsd", [1])
    cvbre_d = dti("cvb_re", [128, 2048]); cvbim_d = dti("cvb_im", [128, 2048])
    ccre_d = dti("cc_re", [128, 32]); ccim_d = dti("cc_im", [128, 32])
    ftd = dti("ftab", [64, 448])
    jd = dti("J128", [128, 128]); eyed = dti("eye128", [128, 128])
    onesrd = dti("ones_row", [1, 128]); onescd = dti("ones_col", [128, 1])
    exp4d = dti("EXP4", [4, 32])
    out_d = dti("o", [ntok, D4], F32, kind="ExternalOutput")
    scr = dti("scr", [20, D4], F32, kind="Internal")
    wscr = dti("wscr", [8, 128 * NCH * 512], BF16, kind="Internal")
    kpad = dti("kpad", [4224], F32, kind="Internal")

    with contextlib.ExitStack() as st:
        sy = Sync(nc, st)
        T, V, S, G, Y = (sy.eng[k] for k in "TVSGY")

        def sb(name, shp, dtype):
            return st.enter_context(nc.sbuf_tensor(name, shp, dtype))

        ps = [st.enter_context(nc.psum_tensor(f"ps{i}", [128, TT], F32)) for i in range(8)]

        def vch(fn, *a, **kw):
            t = sy.inc("V", fn(*a, **kw))
            sy.wait("V", t)
            return t

        def sch(fn, *a, **kw):
            t = sy.inc("S", fn(*a, **kw))
            sy.wait("S", t)
            return t

        for k in ["GS", "GP", "GG", "GU0", "GU1", "GW0", "GW1", "GH0", "GH1",
                  "YB", "YS0", "YS1", "YO0", "YO1", "YW0", "YW1"]:
            sy.new_sem(k)

        def gdma(key, out_ap, in_ap):
            return sy.inc(key, G.dma_start(out=out_ap, in_=in_ap), 16)

        def ydma(key, out_ap, in_ap):
            return sy.inc(key, Y.dma_start(out=out_ap, in_=in_ap), 16)

        uTb = sb("uTb", [128, 16 * TT], BF16)
        onesr = sb("onesr", [1, 128], F32)
        onesc = sb("onesc", [128, 1], F32)
        onescb = sb("onescb", [128, 1], BF16)
        smallcol = sb("smallcol", [128, 16], F32)
        dls = sb("dls", [1, 4], F32)
        sc1 = sb("sc1", [1, 4], F32)
        opdcol = sb("opdcol", [128, 1], F32)
        ztile = sb("ztile", [1, 128], F32)
        Vstack = sb("Vstack", [64, 8], F32)
        tc = sb("tc", [64, 2], F32)
        vct_re = sb("vct_re", [64, 64], F32); vct_im = sb("vct_im", [64, 64], F32)
        ccre = sb("ccre", [128, 32], F32); ccim = sb("ccim", [128, 32], F32)
        ft = sb("ft", [64, 448], F32)
        biasrb2 = sb("biasrb2", [1, D4], BF16)
        onesrb = sb("onesrb", [1, 128], BF16)
        for i, n in enumerate(["lam_re", "lam_im", "p_re", "p_im", "q_re",
                               "q_im", "ct_re", "ct_im", "Bpv"]):
            gdma("GS", smallcol[:, i:i + 1], smd[n].ap())
        gdma("GS", dls[0:1, 0:1], Dv.ap())
        gdma("GS", dls[0:1, 1:2], lsd.ap())
        gdma("GS", vct_re[:, :], vctre_d.ap())
        gdma("GS", vct_im[:, :], vctim_d.ap())
        gdma("GS", ccre[:, :], ccre_d.ap())
        gdma("GS", ccim[:, :], ccim_d.ap())
        gdma("GS", ft[:, :], ftd.ap())
        gdma("GS", onesr[0:1, :], onesrd.ap())
        gdma("GS", onesc[:, :], onescd.ap())
        gdma("GS", onescb[:, :], onescd.ap())
        tk_bb = gdma("GS", biasrb2[0:1, :], biasd.ap())
        gdma("GS", onesrb[0:1, :], onesrd.ap())
        t_bB = tk_bb

        tk_setup = sy.tok("GS")
        tk_small = [tk_setup] * 9
        tk_D = tk_ls = tk_vct = tk_ft = tk_or = tk_oc = tk_ocb = tk_setup
        tk_cc = [tk_setup]

        def u_dma(t, q, war_toks):
            sy.waits("G", war_toks)
            return gdma(f"GU{q % 2}", uTb[:, 8 * (q % 2) * TT: (8 * (q % 2) + 8) * TT],
                        AP(uT.ap().tensor, 8 * q * 128 * ntok + t * TT,
                           [[ntok, 128], [128 * ntok, 8], [1, TT]]))

        sy.waits("S", [tk_D, tk_ls])
        sch(S.activation, sc1[0:1, 0:1], dls[0:1, 1:2], AF.Exp, scale=-1.0)
        t_twod = sch(S.mul, sc1[0:1, 1:2], sc1[0:1, 0:1], 2.0)
        t_opd = sch(S.add, sc1[0:1, 2:3], dls[0:1, 0:1], 1.0)


        sy.waits("T", [tk_or, t_twod, t_opd])
        g1t = sy.inc("T", T.matmul(ps[0][0:128, 0:1], onesr[0:1, 0:128], sc1[0:1, 1:2], start=True, stop=True))
        g2t = sy.inc("T", T.matmul(ps[0][0:128, 1:2], onesr[0:1, 0:128], sc1[0:1, 2:3], start=True, stop=True))
        sy.waits("V", [g1t, g2t])
        t_twocol = vch(V.tensor_copy, smallcol[:, 9:10], ps[0][0:128, 0:1])
        t_opdcol = vch(V.tensor_copy, opdcol[:, :], ps[0][0:128, 1:2])

        sy.waits("T", [tk_vct, tk_small[8], t_twocol, t_opdcol])
        bt1 = sy.inc("T", T.matmul(ps[0][0:64, 0:1], vct_re[:, :], smallcol[0:64, 8:9], start=True, stop=True))
        bt2 = sy.inc("T", T.matmul(ps[0][0:64, 1:2], vct_im[:, :], smallcol[0:64, 8:9], start=True, stop=True))
        sy.waits("V", [bt1, bt2])
        vch(V.tensor_copy, smallcol[0:64, 10:12], ps[0][0:64, 0:2])

        sy.waits("V", [tk_small[7], tk_small[5]])
        vch(V.tensor_scalar_mul, smallcol[0:64, 12:13], smallcol[0:64, 7:8], -1.0)
        vch(V.tensor_scalar_mul, smallcol[0:64, 13:14], smallcol[0:64, 5:6], -1.0)
        sy.waits("V", tk_small[:7])

        BRE, BIM = smallcol[0:64, 10:11], smallcol[0:64, 11:12]
        PRE, PIM = smallcol[0:64, 2:3], smallcol[0:64, 3:4]
        CTR, CTI = smallcol[0:64, 6:7], smallcol[0:64, 12:13]
        QR, QI = smallcol[0:64, 4:5], smallcol[0:64, 13:14]

        def cmul_into(dre, dim_, ar, ai, br, bi):
            vch(V.tensor_mul, tc[:, 0:1], ar, br)
            vch(V.tensor_mul, tc[:, 1:2], ai, bi)
            vch(V.tensor_sub, dre, tc[:, 0:1], tc[:, 1:2])
            vch(V.tensor_mul, tc[:, 0:1], ar, bi)
            vch(V.tensor_mul, tc[:, 1:2], ai, br)
            vch(V.tensor_add, dim_, tc[:, 0:1], tc[:, 1:2])

        cmul_into(Vstack[:, 0:1], Vstack[:, 1:2], CTR, CTI, BRE, BIM)
        cmul_into(Vstack[:, 2:3], Vstack[:, 3:4], CTR, CTI, PRE, PIM)
        cmul_into(Vstack[:, 4:5], Vstack[:, 5:6], QR, QI, BRE, BIM)
        cmul_into(Vstack[:, 6:7], Vstack[:, 7:8], QR, QI, PRE, PIM)
        t_vstack = sy.tok("V")

        win_sb = sb("win_sb", [128, 256], BF16)
        wout_sb = sb("wout_sb", [128, 256], BF16)
        pstat = sb("pstat", [128, 128], BF16)
        band0b = sb("band0b", [128, 128], BF16)
        band1b = sb("band1b", [128, 128], BF16)
        eyeop = sb("eyeop", [128, 128], BF16)
        carry = [sb(f"carry{i}", [128, TT], BF16) for i in range(2)]
        jj = sb("jj", [128, 128], F32)
        hankb = [sb(f"hank{i}", [128, 128], F32) for i in range(2)]
        sy.waits("G", [sy.tok("V"), sy.tok("S")])
        tk_j = gdma("GH0", jj[:, :], jd.ap())

        tk_u0 = None
        tk_u0b = None
        t_band0 = None
        with contextlib.ExitStack() as st2:
            def sb2(name, shp, dtype=F32):
                return st2.enter_context(nc.sbuf_tensor(name, shp, dtype))

            def kpath():
                nonlocal tk_u0, tk_u0b, t_band0
                g1 = sb2("g1", [128, 2048], F32)
                g2 = sb2("g2", [128, 2048], F32)
                g3 = sb2("g3", [128, 2048], F32)
                g4 = sb2("g4", [128, 2048], F32)
                lncol = sb2("lncol", [128, 1], F32)
                vs128 = sb2("vs128", [128, 16], F32)
                stage8 = [sb2(f"stage8{i}", [16, 512], F32) for i in range(2)]
                stage_tok = [None, None]
                t_vz = vch(V.memset, vs128[:, :], 0.0)
                sy.waits("Y", [t_vstack, t_vz])
                yv1 = ydma("YB", AP(scr.ap().tensor, 0, [[1, 512]]), Vstack[:, :])
                sy.waits("Y", [yv1])
                ydma("YB", vs128[0:64, 0:8], AP(scr.ap().tensor, 0, [[8, 64], [1, 8]]))
                yv3 = ydma("YB", vs128[64:128, 8:16], AP(scr.ap().tensor, 0, [[8, 64], [1, 8]]))
                sy.waits("G", [sy.tok("T")])
                tg1 = gdma("GG", g1[:, :], cvbre_d.ap())
                tg2 = gdma("GG", g2[:, :], cvbim_d.ap())
                tk_u0 = u_dma(0, 0, [])
                tk_u0b = u_dma(0, 1, [])
                yield
                sy.waits("V", [tg1, tg2, t_twocol, tk_small[0], tk_small[1]])
                vch(V.tensor_scalar_mul, g1[:, :], g1[:, :], smallcol[:, 9:10])
                vch(V.tensor_scalar_sub, g1[:, :], g1[:, :], smallcol[:, 0:1])
                vch(V.tensor_scalar_mul, g2[:, :], g2[:, :], smallcol[:, 9:10])
                v4 = vch(V.tensor_scalar_sub, g2[:, :], g2[:, :], smallcol[:, 1:2])
                yield
                t_lnc = vch(V.memset, lncol[:, :], -55.451774444795625)
                sy.waits("S", [v4, t_lnc])
                sq1 = sch(S.activation, g3[:, :], g1[:, :], AF.Square, scale=2.0 ** -40)
                sq2 = sch(S.activation, g4[:, :], g2[:, :], AF.Square, scale=2.0 ** -40)
                yield
                sy.waits("V", [sq1, sq2])
                v5 = vch(V.tensor_add, g3[:, :], g3[:, :], g4[:, :])
                sy.wait("S", v5)
                sch(S.activation, g3[:, :], g3[:, :], AF.Ln)
                s4t = sch(S.activation, g3[:, :], g3[:, :], AF.Exp, scale=-1.0, bias=lncol[:, :])
                s5t = sch(S.mul, g2[:, :], g2[:, :], -1.0)
                yield
                sy.waits("V", [s4t, s5t])
                v6 = vch(V.tensor_mul, g1[:, :], g1[:, :], g3[:, :])
                v7 = vch(V.tensor_mul, g2[:, :], g2[:, :], g3[:, :])
                yield
                sy.waits("T", [v6, v7, yv3, t_bB])
                gbank_t = {}
                for c in range(4):
                    csl = slice(c * 512, (c + 1) * 512)
                    bank = ps[6 + c % 2]
                    for half, iv in ((0, g1), (1, g2)):
                        sg = half
                        sy.waits("T", [gbank_t.get(id(bank))])
                        m1 = sy.inc("T", T.matmul(bank[0:16, 0:512], vs128[:, :], iv[:, csl],
                                                  start=True, stop=True))
                        sy.waits("V", [m1, stage_tok[sg]])
                        sy.waits("Y", [yv3])
                        e1 = vch(V.tensor_copy, stage8[sg][:, :], bank[0:16, 0:512])
                        gbank_t[id(bank)] = e1
                        sy.wait("Y", e1)
                        base = (8 * D4 if half == 1 else 0) + c * 512
                        dtok = ydma(f"YS{sg}", AP(scr.ap().tensor, base, [[2048, 2], [D4, 8], [1, 512]]),
                                    stage8[sg][:, :])
                        stage_tok[sg] = dtok
                    yield

                kk = sb2("kk", [128, 512], F32)
                sy.waits("Y", stage_tok)
                b3 = ydma("YB", kk[:, :], AP(scr.ap().tensor, 0, [[32, 128], [D4, 16], [1, 32]]))

                ta = sb2("ta", [128, 8 * 32], F32)
                tb_ = sb2("tb_", [128, 6 * 32], F32)
                kb = lambda r: kk[:, r * 32:(r + 1) * 32]
                ab = lambda r: ta[:, r * 32:(r + 1) * 32]
                tbb = lambda r: tb_[:, r * 32:(r + 1) * 32]
                sy.wait("V", b3)
                yield
                vch(V.tensor_sub, ab(0), kb(0), kb(9));   vch(V.tensor_add, ab(1), kb(1), kb(8))
                vch(V.tensor_sub, ab(2), kb(2), kb(11));  vch(V.tensor_add, ab(3), kb(3), kb(10))
                vch(V.tensor_sub, ab(4), kb(4), kb(13));  vch(V.tensor_add, ab(5), kb(5), kb(12))
                vch(V.tensor_sub, ab(6), kb(6), kb(15));  t_ab7 = vch(V.tensor_add, ab(7), kb(7), kb(14))
                t_ab6 = vch(V.tensor_scalar_add, ab(6), ab(6), 1.0)
                yield
                sy.waits("S", [t_ab6, t_ab7])
                w1 = sch(S.square, tbb(0), ab(6))
                w2 = sch(S.square, tbb(1), ab(7))
                sy.waits("V", [w1, w2])
                w3 = vch(V.tensor_add, tbb(0), tbb(0), tbb(1))
                sy.wait("S", w3)
                sch(S.activation, tbb(0), tbb(0), AF.Ln)
                w4 = sch(S.activation, tbb(0), tbb(0), AF.Exp, scale=-1.0)
                w5 = sch(S.mul, ab(7), ab(7), -1.0)
                yield
                sy.waits("V", [w4, w5])
                vch(V.tensor_mul, ab(6), ab(6), tbb(0))
                vch(V.tensor_mul, ab(7), ab(7), tbb(0))
                vch(V.tensor_mul, tbb(1), ab(2), ab(4))
                vch(V.tensor_mul, tbb(2), ab(3), ab(5))
                vch(V.tensor_sub, tbb(1), tbb(1), tbb(2))
                vch(V.tensor_mul, tbb(2), ab(2), ab(5))
                vch(V.tensor_mul, tbb(3), ab(3), ab(4))
                vch(V.tensor_add, tbb(2), tbb(2), tbb(3))
                yield
                vch(V.tensor_mul, tbb(3), tbb(1), ab(6))
                vch(V.tensor_mul, tbb(4), tbb(2), ab(7))
                vch(V.tensor_sub, tbb(3), tbb(3), tbb(4))
                vch(V.tensor_mul, tbb(4), tbb(1), ab(7))
                vch(V.tensor_mul, tbb(5), tbb(2), ab(6))
                vch(V.tensor_add, tbb(4), tbb(4), tbb(5))
                vch(V.tensor_sub, ab(0), ab(0), tbb(3))
                vch(V.tensor_sub, ab(1), ab(1), tbb(4))
                yield
                sy.waits("V", tk_cc)
                vch(V.tensor_mul, tbb(0), ccre[:, :], ab(0))
                vch(V.tensor_mul, tbb(1), ccim[:, :], ab(1))
                vch(V.tensor_sub, tbb(0), tbb(0), tbb(1))
                vch(V.tensor_mul, tbb(1), ccre[:, :], ab(1))
                vch(V.tensor_mul, tbb(2), ccim[:, :], ab(0))
                t_ar = vch(V.tensor_add, tbb(1), tbb(1), tbb(2))
                yield

                sy.wait("Y", t_ar)
                b4 = ydma("YB", AP(scr.ap().tensor, 16 * D4, [[1, D4]]), tbb(0))
                b5 = ydma("YB", AP(scr.ap().tensor, 17 * D4, [[1, D4]]), tbb(1))
                Rre = sb2("Rre", [64, 64], F32)
                Rim = sb2("Rim", [64, 64], F32)
                sy.waits("Y", [b4, b5])
                b6 = ydma("YB", Rre[:, :], AP(scr.ap().tensor, 16 * D4, [[64, 64], [1, 64]]))
                b7 = ydma("YB", Rim[:, :], AP(scr.ap().tensor, 17 * D4, [[64, 64], [1, 64]]))
                yield

                FRE, FIM, FIMN = ft[:, 0:64], ft[:, 64:128], ft[:, 128:192]
                FCRE, FCIMN = ft[:, 192:256], ft[:, 256:320]
                TRE, TIM = ft[:, 320:384], ft[:, 384:448]
                sy.waits("T", [b6, b7, tk_ft, gbank_t.get(id(ps[6]))])
                sy.inc("T", T.matmul(ps[6][0:64, 0:64], Rre[:, :], FRE, start=True, stop=False))
                am1 = sy.inc("T", T.matmul(ps[6][0:64, 0:64], Rim[:, :], FIMN, start=False, stop=True))
                sy.inc("T", T.matmul(ps[6][0:64, 64:128], Rre[:, :], FIM, start=True, stop=False))
                am2 = sy.inc("T", T.matmul(ps[6][0:64, 64:128], Rim[:, :], FRE, start=False, stop=True))
                a1sb = sb2("a1sb", [64, 128], F32)
                a2sb = sb2("a2sb", [64, 128], F32)
                t64a = sb2("t64a", [64, 64], F32)
                t64b = sb2("t64b", [64, 64], F32)
                sy.waits("V", [am1, am2])
                vch(V.tensor_copy, a1sb[:, :], ps[6][0:64, 0:128])
                vch(V.tensor_mul, t64a[:, :], a1sb[:, 0:64], TRE)
                vch(V.tensor_mul, t64b[:, :], a1sb[:, 64:128], TIM)
                vch(V.tensor_sub, a2sb[:, 0:64], t64a[:, :], t64b[:, :])
                vch(V.tensor_mul, t64a[:, :], a1sb[:, 0:64], TIM)
                vch(V.tensor_mul, t64b[:, :], a1sb[:, 64:128], TRE)
                t_a2 = vch(V.tensor_add, a2sb[:, 64:128], t64a[:, :], t64b[:, :])
                yield
                sy.wait("T", t_a2)
                sy.inc("T", T.matmul(ps[7][0:64, 0:64], FCRE, a2sb[:, 0:64], start=True, stop=False))
                kmt = sy.inc("T", T.matmul(ps[7][0:64, 0:64], FCIMN, a2sb[:, 64:128], start=False, stop=True))
                Ksb = sb2("Ksb", [64, 64], F32)
                sy.wait("V", kmt)
                t_ksb = vch(V.tensor_copy, Ksb[:, :], ps[7][0:64, 0:64])
                yield

                t_z = vch(V.memset, ztile[0:1, :], 0.0)
                sy.waits("Y", [t_z, t_ksb])
                z1 = ydma("YB", AP(kpad.ap().tensor, 0, [[1, 127]]), ztile[0:1, 0:127])
                z2 = ydma("YB", AP(kpad.ap().tensor, 127, [[1, D4]]), Ksb[:, :])
                z3 = ydma("YB", AP(kpad.ap().tensor, 4223, [[1, 1]]), ztile[0:1, 0:1])
                tk_kpad = [z1, z2, z3]
                yield

                sy.waits("G", tk_kpad)
                hg = gdma("GH0", hankb[0][:, :], AP(kpad.ap().tensor, 0, [[1, 128], [1, 128]]))
                hg1 = gdma("GH0", hankb[1][:, :], AP(kpad.ap().tensor, 128, [[1, 128], [1, 128]]))
                hg = hg1
                sy.waits("T", [hg, tk_j])
                bm = sy.inc("T", T.matmul(ps[4][0:128, 0:128], jj[:, :], hankb[0][:, :], start=True, stop=True))
                sy.wait("S", bm)
                t_band0 = sch(S.copy, band0b[:, :], ps[4][0:128, 0:128])
                sy.waits("T", [hg1])
                bm1 = sy.inc("T", T.matmul(ps[5][0:128, 0:128], jj[:, :], hankb[1][:, :], start=True, stop=True))
                sy.wait("S", bm1)
                t_band0 = sch(S.copy, band1b[:, :], ps[5][0:128, 0:128])

                yield

            kg = kpath()
            kdone = [False]

            def kstep(n=1):
                for _ in range(n):
                    if kdone[0]:
                        return
                    if next(kg, "END") == "END":
                        kdone[0] = True

            kstep(2)
            col2 = sb2("col2", [64, 32])
            srow = sb2("srow", [1, 16])
            rows8 = sb2("rows8", [1, 768])
            eye64f = sb2("eye64f", [64, 64])
            osb = sb2("osb", [64, 128])
            dgm = sb2("dgm", [64, 64])
            t64c = sb2("t64c", [64, 64])
            t64d = sb2("t64d", [64, 64])
            ycol = sb2("ycol", [64, 512])
            crow = sb2("crow", [64, 512])
            ytr = sb2("ytr", [128, 64])
            mats = {}

            def cm(name):
                mats[name] = sb2(name, [64, 192])
                return mats[name]

            def mre(n): return mats[n][:, 0:64]
            def mim(n): return mats[n][:, 64:128]
            def mimN(n): return mats[n][:, 128:192]

            sy.waits("G", [sy.tok("V"), sy.tok("S")])
            ge64 = gdma("GH0", eye64f[:, :], eyed.ap()[0:64, 0:64])

            C = lambda i: col2[:, i:i + 1]
            SM = lambda i: smallcol[0:64, i:i + 1]
            SR = lambda i: srow[0:1, i:i + 1]
            t_dl = sch(S.activation, SR(0), dls[0:1, 1:2], AF.Exp)
            t_hd = sch(S.mul, SR(1), SR(0), 0.5)
            sy.waits("T", [t_hd, tk_or, sy.tok("V")])
            hm = sy.inc("T", T.matmul(ps[0][0:64, 0:1], onesr[0:1, 0:64], SR(1), start=True, stop=True))
            sy.wait("V", hm)
            vch(V.tensor_copy, C(0), ps[0][0:64, 0:1])
            vch(V.tensor_scalar_mul, C(1), SM(0), C(0))
            vch(V.tensor_scalar_mul, C(1), C(1), -1.0)
            vch(V.tensor_scalar_add, C(1), C(1), 1.0)
            vch(V.tensor_scalar_mul, C(2), SM(1), C(0))
            vch(V.tensor_scalar_mul, C(2), C(2), -1.0)
            vch(V.tensor_mul, C(3), C(1), C(1))
            vch(V.tensor_mul, C(4), C(2), C(2))
            vch(V.tensor_add, C(3), C(3), C(4))
            vch(V.reciprocal, C(4), C(3))
            vch(V.tensor_mul, C(5), C(1), C(4))
            vch(V.tensor_mul, C(6), C(2), C(4))
            vch(V.tensor_scalar_mul, C(6), C(6), -1.0)

            def cmul_cols(dr, di, ar, ai, br, bi):
                vch(V.tensor_mul, C(14), ar, br)
                vch(V.tensor_mul, C(15), ai, bi)
                vch(V.tensor_sub, dr, C(14), C(15))
                vch(V.tensor_mul, C(14), ar, bi)
                vch(V.tensor_mul, C(15), ai, br)
                vch(V.tensor_add, di, C(14), C(15))

            cmul_cols(C(7), C(8), C(5), C(6), SM(2), SM(3))
            cmul_cols(C(9), C(10), SM(4), SM(13), C(5), C(6))
            cmul_cols(C(11), C(12), C(7), C(8), SM(4), SM(13))
            t_w = sy.tok("V")
            sy.waits("T", [t_w, tk_oc])
            sm0 = sy.inc("T", T.matmul(ps[0][0:1, 0:2], onesc[0:64, 0:1], col2[:, 11:13], start=True, stop=True))
            sy.wait("V", sm0)
            vch(V.tensor_copy, srow[0:1, 2:4], ps[0][0:1, 0:2])
            vch(V.tensor_mul, SR(4), SR(1), SR(2))
            vch(V.tensor_scalar_add, SR(4), SR(4), 1.0)
            vch(V.tensor_mul, SR(5), SR(1), SR(3))
            vch(V.tensor_mul, SR(6), SR(4), SR(4))
            vch(V.tensor_mul, SR(7), SR(5), SR(5))
            vch(V.tensor_add, SR(6), SR(6), SR(7))
            vch(V.reciprocal, SR(7), SR(6))
            vch(V.tensor_mul, SR(8), SR(4), SR(7))
            vch(V.tensor_mul, SR(8), SR(8), SR(1))
            vch(V.tensor_mul, SR(9), SR(5), SR(7))
            vch(V.tensor_mul, SR(9), SR(9), SR(1))
            t_s = vch(V.tensor_scalar_mul, SR(9), SR(9), -1.0)
            sy.waits("T", [t_s])
            sb1 = sy.inc("T", T.matmul(ps[0][0:64, 2:4], onesr[0:1, 0:64], srow[0:1, 8:10], start=True, stop=True))
            sy.wait("V", sb1)
            vch(V.tensor_copy, col2[:, 16:18], ps[0][0:64, 2:4])
            vch(V.tensor_scalar_mul, C(18), C(8), -1.0)
            vch(V.tensor_scalar_mul, C(19), C(10), -1.0)
            vch(V.tensor_scalar_mul, C(20), SM(3), -1.0)
            t_cols = vch(V.tensor_scalar_mul, C(21), SM(13), -1.0)
            sy.wait("Y", t_cols)
            yb = None
            rowsrc = [C(7), C(8), C(9), C(10), SM(2), SM(3), SM(4), SM(13),
                      C(18), C(19), C(20), C(21)]
            sy.waits("Y", [sy.tok("YB")])
            for i, src in enumerate(rowsrc):
                yb = ydma("YB", AP(scr.ap().tensor, 18 * D4 + 64 * i, [[1, 64]]), src)
            sy.waits("Y", [yb])
            yb2 = ydma("YB", rows8[0:1, :], AP(scr.ap().tensor, 18 * D4, [[1, 768]]))
            R = lambda i: rows8[0:1, i * 64:(i + 1) * 64]
            ARE, AIM, BRE2, BIM2 = R(0), R(1), R(2), R(3)
            PRE2, PIM2, QBR, QBI = R(4), R(5), R(6), R(7)
            AIMN, BIMN, PIMN, QBIN = R(8), R(9), R(10), R(11)

            bank_t = {}
            mat_t = {}

            def outer_pair(bank, ur, uim, uimN, vr, vi, extra=()):
                sy.waits("T", [yb2, *extra, bank_t.get(id(bank))])
                T.matmul(bank[0:64, 0:64], ur, vr, start=True, stop=False)
                T.matmul(bank[0:64, 0:64], uimN, vi, start=False, stop=True)
                T.matmul(bank[0:64, 64:128], ur, vi, start=True, stop=False)
                mm4 = sy.inc("T", T.matmul(bank[0:64, 64:128], uim, vr, start=False, stop=True))
                return mm4

            def build_M(name, dcol_re, dcol_im, scol_re_ap, scol_im_ap,
                        ur, uim, uimN, vr, vi):
                mm4 = outer_pair(ps[1], ur, uim, uimN, vr, vi)
                sy.wait("V", mm4)
                ev = vch(V.tensor_copy, osb[:, :], ps[1][0:64, 0:128])
                bank_t[id(ps[1])] = ev
                X = cm(name)
                vch(V.tensor_scalar_mul, dgm[:, :], eye64f[:, :], dcol_re)
                vch(V.tensor_scalar_mul, t64c[:, :], osb[:, 0:64], scol_re_ap)
                vch(V.tensor_scalar_mul, t64d[:, :], osb[:, 64:128], scol_im_ap)
                vch(V.tensor_sub, t64c[:, :], t64c[:, :], t64d[:, :])
                vch(V.tensor_sub, X[:, 0:64], dgm[:, :], t64c[:, :])
                vch(V.tensor_scalar_mul, dgm[:, :], eye64f[:, :], dcol_im)
                vch(V.tensor_scalar_mul, t64c[:, :], osb[:, 64:128], scol_re_ap)
                vch(V.tensor_scalar_mul, t64d[:, :], osb[:, 0:64], scol_im_ap)
                vch(V.tensor_add, t64c[:, :], t64c[:, :], t64d[:, :])
                vch(V.tensor_sub, X[:, 64:128], dgm[:, :], t64c[:, :])
                t = vch(V.tensor_scalar_mul, X[:, 128:192], X[:, 64:128], -1.0)
                return t

            sy.waits("V", [ge64, yb2])
            vch(V.tensor_scalar_mul, C(22), SM(0), C(0))
            vch(V.tensor_scalar_add, C(22), C(22), 1.0)
            vch(V.tensor_scalar_mul, C(23), SM(1), C(0))
            vch(V.tensor_scalar_mul, C(31), C(0), 0.0)
            mat_t["M1T"] = build_M("M1T", C(5), C(6), col2[:, 16:17], col2[:, 17:18], BRE2, BIM2, BIMN, ARE, AIM)
            mat_t["M1"] = build_M("M1", C(5), C(6), col2[:, 16:17], col2[:, 17:18], ARE, AIM, AIMN, BRE2, BIM2)
            mat_t["M2"] = build_M("M2", C(22), C(23), C(0), C(31), PRE2, PIM2, PIMN, QBR, QBI)
            mat_t["M2T"] = build_M("M2T", C(22), C(23), C(0), C(31), QBR, QBI, QBIN, PRE2, PIM2)
            kstep(2)

            def cmm(dst_name, statT, movn, w=64, dst=None, want_imN=True, bank=None):
                if bank is None:
                    bank = ps[2]
                movre, movim = mre(movn), mim(movn)
                sy.waits("T", [mat_t.get(statT), mat_t.get(movn), bank_t.get(id(bank))])
                T.matmul(bank[0:64, 0:w], mre(statT), movre, start=True, stop=False)
                T.matmul(bank[0:64, 0:w], mimN(statT), movim, start=False, stop=True)
                T.matmul(bank[0:64, w:2 * w], mre(statT), movim, start=True, stop=False)
                mm4 = sy.inc("T", T.matmul(bank[0:64, w:2 * w], mim(statT), movre, start=False, stop=True))
                sy.wait("S", mm4)
                if dst is None:
                    X = cm(dst_name)
                    ev = sch(S.copy, X[:, 0:128], bank[0:64, 0:128])
                    if want_imN:
                        ev = sch(S.mul, X[:, 128:192], X[:, 64:128], -1.0)
                else:
                    ev = sch(S.copy, dst, bank[0:64, 0:2 * w])
                bank_t[id(bank)] = ev
                mat_t[dst_name] = ev
                return ev

            cmm("P0", "M1T", "M2")
            cmm("P0T", "M2", "M1T", bank=ps[3])
            sy.waits("T", [mat_t.get("P0"), bank_t.get(id(ps[1]))])
            T.matmul(ps[1][0:64, 0:1], mre("P0"), SM(6), start=True, stop=False)
            T.matmul(ps[1][0:64, 0:1], mimN("P0"), SM(12), start=False, stop=True)
            T.matmul(ps[1][0:64, 1:2], mre("P0"), SM(12), start=True, stop=False)
            cm4 = sy.inc("T", T.matmul(ps[1][0:64, 1:2], mim("P0"), SM(6), start=False, stop=True))
            sy.wait("S", cm4)
            sch(S.copy, crow[:, 0:1], ps[1][0:64, 0:1])
            t_cr = sch(S.copy, crow[:, 256:257], ps[1][0:64, 1:2])
            bank_t[id(ps[1])] = t_cr

            def wout_step(j):
                nonlocal t_cr
                w = 1 << j
                bank = ps[1]
                sy.waits("T", [mat_t.get(f"P{j}"), t_cr, bank_t.get(id(bank))])
                T.matmul(bank[0:64, 0:w], mre(f"P{j}"), crow[:, 0:w], start=True, stop=False)
                T.matmul(bank[0:64, 0:w], mimN(f"P{j}"), crow[:, 256:256 + w], start=False, stop=True)
                T.matmul(bank[0:64, w:2 * w], mre(f"P{j}"), crow[:, 256:256 + w], start=True, stop=False)
                ym = sy.inc("T", T.matmul(bank[0:64, w:2 * w], mim(f"P{j}"), crow[:, 0:w], start=False, stop=True))
                sy.wait("S", ym)
                sch(S.copy, crow[:, w:2 * w], bank[0:64, 0:w])
                t_cr = sch(S.copy, crow[:, 256 + w:256 + 2 * w], bank[0:64, w:2 * w])
                bank_t[id(bank)] = t_cr

            for j in range(1, 16):
                cmm(f"P{j}", f"P{j-1}T", f"P{j-1}")
                kstep()
                cmm(f"P{j}T", f"P{j-1}", f"P{j-1}T", bank=ps[3])
                kstep()
                if j - 1 <= 7:
                    wout_step(j - 1)

            sy.waits("T", [t_dl])
            dm_ = sy.inc("T", T.matmul(ps[0][0:64, 4:5], onesr[0:1, 0:64], SR(0), start=True, stop=True))
            sy.wait("V", dm_)
            vch(V.tensor_copy, C(26), ps[0][0:64, 4:5])
            vch(V.tensor_mul, C(24), SM(10), C(26))
            t_bd = vch(V.tensor_mul, C(25), SM(11), C(26))
            bank = ps[3]
            sy.waits("T", [t_bd, mat_t.get("M1T"), bank_t.get(id(bank))])
            T.matmul(bank[0:64, 0:1], mre("M1T"), C(24), start=True, stop=False)
            T.matmul(bank[0:64, 0:1], mimN("M1T"), C(25), start=False, stop=True)
            T.matmul(bank[0:64, 1:2], mre("M1T"), C(25), start=True, stop=False)
            bb4 = sy.inc("T", T.matmul(bank[0:64, 1:2], mim("M1T"), C(24), start=False, stop=True))
            sy.wait("S", bb4)
            sch(S.copy, col2[:, 27:29], bank[0:64, 0:2])
            t_neu = sch(S.copy, col2[:, 29:31], col2[:, 27:29])
            for it_ in range(16):
                sy.waits("T", [mat_t.get("P12T"), t_neu, bank_t.get(id(bank))])
                T.matmul(bank[0:64, 2:3], mre("P12T"), col2[:, 29:30], start=True, stop=False)
                T.matmul(bank[0:64, 2:3], mimN("P12T"), col2[:, 30:31], start=False, stop=True)
                T.matmul(bank[0:64, 3:4], mre("P12T"), col2[:, 30:31], start=True, stop=False)
                nm4 = sy.inc("T", T.matmul(bank[0:64, 3:4], mim("P12T"), col2[:, 29:30], start=False, stop=True))
                sy.wait("S", nm4)
                sch(S.activation, col2[:, 29:30], ps[3][0:64, 2:3], AF.Identity, bias=col2[:, 27:28])
                ev = sch(S.activation, col2[:, 30:31], ps[3][0:64, 3:4], AF.Identity, bias=col2[:, 28:29])
                bank_t[id(bank)] = ev
                t_neu = ev
                kstep()

            sch(S.copy, ycol[:, 0:1], col2[:, 29:30])
            t_yc = sch(S.copy, ycol[:, 256:257], col2[:, 30:31])
            for j in range(8):
                w = 1 << j
                bank = ps[0]
                sy.waits("T", [mat_t.get(f"P{j}T"), t_yc, bank_t.get(id(bank))])
                T.matmul(bank[0:64, 0:w], mre(f"P{j}T"), ycol[:, 0:w], start=True, stop=False)
                T.matmul(bank[0:64, 0:w], mimN(f"P{j}T"), ycol[:, 256:256 + w], start=False, stop=True)
                T.matmul(bank[0:64, w:2 * w], mre(f"P{j}T"), ycol[:, 256:256 + w], start=True, stop=False)
                ym = sy.inc("T", T.matmul(bank[0:64, w:2 * w], mim(f"P{j}T"), ycol[:, 0:w], start=False, stop=True))
                sy.wait("S", ym)
                sch(S.copy, ycol[:, w:2 * w], bank[0:64, 0:w])
                t_yc = sch(S.copy, ycol[:, 256 + w:256 + 2 * w], bank[0:64, w:2 * w])
                bank_t[id(bank)] = t_yc
                kstep()

            for blk, c0 in ((0, 128), (1, 0)):
                for part in range(2):
                    src = ycol[:, 256 * part + c0: 256 * part + c0 + 128]
                    sy.waits("T", [t_yc, bank_t.get(id(ps[2]))])
                    tr = sy.inc("T", T.transpose(ps[2][0:128, 0:64], src, eye64f[:, :]))
                    sy.wait("S", tr)
                    ev = sch(S.copy, ytr[:, :], ps[2][0:128, 0:64])
                    bank_t[id(ps[2])] = ev
                    sy.waits("T", [ev, bank_t.get(id(ps[3])), tk_j])
                    jm = sy.inc("T", T.matmul(ps[3][0:128, 0:64], jj[:, :], ytr[:, :], start=True, stop=True))
                    sy.wait("S", jm)
                    ev2 = sch(S.copy, win_sb[:, 128 * blk + 64 * part: 128 * blk + 64 * part + 64],
                              ps[3][0:128, 0:64])
                    bank_t[id(ps[3])] = ev2
                    kstep()
            sch(S.copy, wout_sb[0:64, 0:128], crow[:, 0:128])
            sch(S.mul, wout_sb[64:128, 0:128], crow[:, 256:384], -1.0)
            sch(S.copy, wout_sb[0:64, 128:256], crow[:, 128:256])
            t_wo = sch(S.mul, wout_sb[64:128, 128:256], crow[:, 384:512], -1.0)
            sch(S.copy, pstat[0:64, 0:64], mre("P8T"))
            sch(S.copy, pstat[0:64, 64:128], mim("P8T"))
            sch(S.mul, pstat[64:128, 0:64], mim("P8T"), -1.0)
            t_ps = sch(S.copy, pstat[64:128, 64:128], mre("P8T"))
            t_zc = vch(V.memset, carry[0][:, :], 0.0)
            while not kdone[0]:
                kstep()
        t_ssm = [t_band0, t_wo, t_ps, t_zc, sy.tok("V"), sy.tok("S")]

        r1b = sb("r1b", [128, NCH * TT], BF16)
        scr16 = sb("scr16", [128, NCH * TT], BF16)
        wtb = [sb(f"wtb{i}", [128, NCH * 512], BF16) for i in range(2)]
        outf = [sb(f"outf{i}", [128, 1024], F32) for i in range(2)]
        catb = sb("catb", [128, NCH * 36], BF16)
        bmtb = sb("bmtb", [32, D4], BF16)
        eyeb = sb("eyeb", [128, 128], BF16)
        exp4sb = sb("exp4sb", [4, 32], F32)
        mB = sb("mB", [128, TT], BF16)
        sigB = sb("sigB", [128, TT], BF16)
        statrow = sb("statrow", [1, 1024], F32)
        m_row = sb("m_row", [1, TT], F32)
        lnt2 = statrow[0:1, 0:TT]
        sigrow = statrow[0:1, TT:2 * TT]
        ginv2 = m_row[0:1, 0:TT]
        bnbuf = sb("bnbuf", [128, 48], F32)
        bnagg4 = sb("bnagg4", [128, 8], F32)
        vcol4 = sb("vcol4", [128, 4], F32)
        sigcol4 = sb("sigcol4", [128, 4], F32)
        nt = sb("nt", [128, 2048], BF16)
        ntf = sb("ntf", [128, 1024], F32)
        projf = ntf[0:32, 0:TT]
        geexp = ntf[0:4, TT:2 * TT]
        gates4 = geexp
        sprojb = nt[0:32, 0:TT]
        sy.waits("G", [sy.tok("V"), sy.tok("S"), sy.tok("T")])
        gdma("GP", eyeb[:, :], eyed.ap())
        gdma("GP", exp4sb[:, :], exp4d.ap())
        gdma("GP", bmtb[:, :], BmTd.ap())
        gdma("GP", catb[:, :], AP(CatA.ap().tensor, 0, [[36, 128], [128 * 36, NCH], [1, 36]]))
        tk_post = sy.tok("GP")
        tk_eye = tk_orb = tk_e4 = tk_biasb = tk_cat = tk_bmt = tk_post

        sy.waits("V", [tk_eye, t_opdcol])
        t_eyeop = vch(V.tensor_scalar_mul, eyeop[:, :], eyeb[:, :], opdcol[:, :])
        sy.waits("V", [t_band0])
        t_bf = vch(V.tensor_add, band0b[:, :], band0b[:, :], eyeop[:, :])
        t_ssm = t_ssm + [t_bf]
        t_postsetup = [sy.tok("V"), sy.tok("S"), sy.tok("T")]

        bank_w = {}
        bank_r = {}
        uq_tok = {0: tk_u0, 1: tk_u0b, 2: None, 3: None}
        ln2_pend = None

        def ln2_stats(pend):
            tp, fV, fS = pend
            sy.waits("V", [fV, fS])
            for tb in range(4):
                base = tb * D4
                for kq in range(8):
                    vch(V.bn_stats, bnbuf[:, kq * 6:(kq + 1) * 6],
                        scr16[:, base + kq * 512: base + (kq + 1) * 512])
                vch(V.bn_aggr, bnagg4[:, 2 * tb:2 * tb + 2], bnbuf[:, :])
            va = AP(bnagg4[:, :].tensor, 1, [[8, 128], [2, 4]])
            return vch(V.tensor_scalar_add, vcol4[:, :], va, EPS)

        def ln2_scal(t_vc):
            sy.wait("S", t_vc)
            sch(S.activation, vcol4[:, :], vcol4[:, :], AF.Ln)
            return sch(S.activation, sigcol4[:, :], vcol4[:, :], AF.Exp, scale=-0.5)

        def ln2_norm(pend, t_sc):
            tp = pend[0]
            sy.wait("V", t_sc)
            for tb in range(4):
                base = tb * D4
                for hf in range(4):
                    op = outf[hf % 2]
                    sy.waits("V", [bank_w.get(("outf", hf % 2))])
                    t_of = vch(V.tensor_scalar, op[:, :],
                               scr16[:, base + hf * 1024: base + (hf + 1) * 1024],
                               bnagg4[:, 2 * tb:2 * tb + 1], sigcol4[:, tb:tb + 1],
                               OP.subtract, op1=OP.mult)
                    sy.wait("Y", t_of)
                    bank_w[("outf", hf % 2)] = ydma(f"YO{hf % 2}",
                                                    AP(out_d.ap().tensor,
                                                       (tp * TT + tb * 128) * D4 + hf * 1024,
                                                       [[D4, 128], [1, 1024]]), op[:, :])
            bank_r["scr16"] = sy.tok("V")

        for t in range(ntile):
            def w_dma(os):
                par = os % 2
                sy.waits("G", [bank_w.get(("wtb", par)), bank_w.get(("wexp", par)),
                               bank_w.get(("wadd", par))] + t_postsetup)
                if t == 0:
                    tk = gdma(f"GW{par}", wtb[par][:, :],
                              AP(WT.ap().tensor, os * 512,
                                 [[D4, 128], [128 * D4, NCH], [1, 512]]))
                    sy.waits("V", [tk, tk_eye])
                    for jjj in range(4):
                        it = os * 4 + jjj
                        dsl = slice(it * 512 + jjj * 128, it * 512 + (jjj + 1) * 128)
                        tk = vch(V.tensor_add, wtb[par][:, dsl], wtb[par][:, dsl], eyeb[:, :])
                    bank_w[("wadd", par)] = tk
                    bank_w[("wdma", os)] = tk
                else:
                    bank_w[("wdma", os)] = gdma(f"GW{par}", wtb[par][:, :], wscr.ap()[os, :])

            t_vc2 = ln2_stats(ln2_pend) if ln2_pend else None

            sy.waits("T", t_ssm + [bank_r.get("carrynxt0")])
            stat_dep = [None] * 16
            stats_fin = [None, None]

            def st_mm(c_):
                evA, sqA, evB, sqB = stat_dep[c_]
                first, last = (c_ == 0), (c_ == 15)
                deps = [evA, evB, sqA, sqB]
                if first:
                    deps += [bank_r.get(6), bank_r.get(7)]
                sy.waits("T", deps)
                for k2 in range(2):
                    dsl = slice((2 * c_ + k2) * TT, (2 * c_ + k2 + 1) * TT)
                    ssl = slice(((2 * c_ + k2) % 16) * TT, ((2 * c_ + k2) % 16 + 1) * TT)
                    mm1 = T.matmul(ps[6][0:1, 0:512], onescb[:, :], r1b[:, dsl],
                                   start=(first and k2 == 0), stop=(last and k2 == 1))
                    mm2 = T.matmul(ps[7][0:1, 0:512], onescb[:, :], uTb[:, ssl],
                                   start=(first and k2 == 0), stop=(last and k2 == 1))
                    if last and k2 == 1:
                        stats_fin[0] = sy.inc("T", mm1)
                        stats_fin[1] = sy.inc("T", mm2)
                    elif k2 == 1:
                        stats_fin[0] = sy.inc("T", mm2)

            for c in range(16):
                cur, nxt = c % 2, (c + 1) % 2
                biA, biB = c % 2, 2 + c % 2
                slA = slice(((2 * c) % 16) * TT, ((2 * c) % 16 + 1) * TT)
                slB = slice(((2 * c + 1) % 16) * TT, ((2 * c + 1) % 16 + 1) * TT)
                rsA = slice((2 * c) * TT, (2 * c + 1) * TT)
                rsB = slice((2 * c + 1) * TT, (2 * c + 2) * TT)
                sy.waits("T", [uq_tok[c // 4]])
                sy.waits("T", [bank_r.get("psC"), bank_r.get(4)])
                T.matmul(ps[4][0:128, 0:512], pstat[:, :], carry[cur][:, :], start=True, stop=False)
                T.matmul(ps[4][0:128, 0:512], win_sb[:, 0:128], uTb[:, slA], start=False, stop=False)
                cmt = sy.inc("T", T.matmul(ps[4][0:128, 0:512], win_sb[:, 128:256], uTb[:, slB], start=False, stop=True))
                bank_w["psC"] = cmt
                sy.waits("T", [bank_r.get(biA)])
                T.matmul(ps[biA][0:128, 0:512], band0b[:, :], uTb[:, slA], start=True, stop=False)
                amt = sy.inc("T", T.matmul(ps[biA][0:128, 0:512], wout_sb[:, 0:128], carry[cur][:, :], start=False, stop=True))
                bank_w[biA] = amt
                sy.waits("T", [bank_r.get(biB)])
                T.matmul(ps[biB][0:128, 0:512], band1b[:, :], uTb[:, slA], start=True, stop=False)
                T.matmul(ps[biB][0:128, 0:512], band0b[:, :], uTb[:, slB], start=False, stop=False)
                bmt = sy.inc("T", T.matmul(ps[biB][0:128, 0:512], wout_sb[:, 128:256], carry[cur][:, :], start=False, stop=True))
                bank_w[biB] = bmt
                if c > 1:
                    st_mm(c - 2)
                sy.waits("S", [cmt, bank_r.get("carrynxt%d" % nxt)])
                cv = sch(S.copy, carry[nxt][:, :], ps[4][0:128, 0:512])
                bank_r["psC"] = cv
                bank_r[4] = cv
                bank_r["carrynxt%d" % nxt] = cv
                sy.waits("S", [amt, bmt])
                evA = sch(S.copy, r1b[:, rsA], ps[biA][0:128, 0:512])
                sqA = sch(S.square, uTb[:, slA], r1b[:, rsA])
                bank_r[biA] = evA
                evB = sch(S.copy, r1b[:, rsB], ps[biB][0:128, 0:512])
                sqB = sch(S.square, uTb[:, slB], r1b[:, rsB])
                bank_r[biB] = evB
                stat_dep[c] = (evA, sqA, evB, sqB)
                if c == 5:
                    uq_tok[2] = u_dma(t, 2, [sy.tok("T"), sy.tok("S"), sy.tok("V")])
                if c == 9:
                    uq_tok[3] = u_dma(t, 3, [sy.tok("T"), sy.tok("S"), sy.tok("V")])
                if c == 11:
                    w_dma(0)
                    w_dma(1)
                if c == 12 and ln2_pend is not None:
                    t_sc2 = ln2_scal(t_vc2)
            st_mm(14)
            st_mm(15)
            t_s1 = stats_fin[0]
            t_s2 = stats_fin[1]
            t_conv = sy.tok("V")
            t_convT = sy.tok("T")
            t_convS = sy.tok("S")
            sy.waits("V", [t_convS, t_convT])
            zc = vch(V.memset, carry[0][:, :], 0.0)
            bank_r["carrynxt0"] = zc

            sy.waits("V", [t_s1, t_s2])
            vch(V.tensor_scalar_mul, m_row[0:1, :], ps[6][0:1, 0:512], 1.0 / D4)
            vch(V.tensor_mul, lnt2, m_row[0:1, :], m_row[0:1, :])
            e_st = vch(V.scalar_tensor_tensor, lnt2, ps[7][0:1, 0:512], 1.0 / D4, lnt2,
                       OP.mult, OP.subtract)
            bank_r[6] = e_st
            bank_r[7] = e_st
            t_var = vch(V.tensor_scalar_add, lnt2, lnt2, EPS)
            sy.wait("S", t_var)
            sch(S.activation, lnt2, lnt2, AF.Ln)
            t_sig = sch(S.activation, sigrow, lnt2, AF.Exp, scale=-0.5)
            sy.waits("T", [bank_r[6], bank_r[7], sy.tok("V"), t_sig])
            m1 = sy.inc("T", T.matmul(ps[6][0:128, 0:512], onesr[0:1, :], m_row[0:1, :], start=True, stop=True))
            m2 = sy.inc("T", T.matmul(ps[7][0:128, 0:512], onesr[0:1, :], sigrow, start=True, stop=True))
            bank_w[6], bank_w[7] = m1, m2
            sy.waits("S", [m1, m2])
            sch(S.copy, mB[:, :], ps[6][0:128, 0:512])
            t_sigB = sch(S.copy, sigB[:, :], ps[7][0:128, 0:512])
            bank_r[6] = t_sigB
            bank_r[7] = t_sigB
            sy.waits("V", [t_sigB, t_s1, t_s2, t_convS])
            mB4 = AP(mB[:, :].tensor, 0, [[512, 128], [0, 4], [1, 512]])
            sigB4 = AP(sigB[:, :].tensor, 0, [[512, 128], [0, 4], [1, 512]])
            sy.waits("T", [bank_r[6], tk_cat, bank_r.get(0), bank_w.get(("wdma", 0))])
            for cq in range(8):
                cs = slice(cq * 2048, (cq + 1) * 2048)
                vch(V.tensor_sub, nt[:, :], r1b[:, cs], mB4)
                tnq = vch(V.tensor_mul, r1b[:, cs], nt[:, :], sigB4)
                sy.waits("T", [tnq])
                for it in range(4 * cq, 4 * cq + 4):
                    mm = T.matmul(ps[6][0:36, 0:512], catb[:, it * 36:(it + 1) * 36],
                                  r1b[:, it * TT:(it + 1) * TT], start=(it == 0), stop=(it == NCH - 1))
                    T.matmul(ps[0][0:128, 0:512], r1b[:, it * TT: it * TT + 128],
                             wtb[0][:, it * 512:(it + 1) * 512], start=(it == 0), stop=False)
            t_pj = sy.inc("T", mm)
            t_x1 = sy.tok("V")
            sy.wait("V", t_pj)
            e_pj = vch(V.tensor_copy, projf, ps[6][0:32, 0:512])
            sy.wait("S", t_pj)
            t_ge = sch(S.activation, geexp, ps[6][32:36, 0:512], AF.Exp)
            bank_r[6] = t_ge
            sy.waits("T", [t_ge, tk_oc, bank_r.get(7)])
            t_gs = sy.inc("T", T.matmul(ps[7][0:1, 0:512], onesc[0:4, 0:1], geexp, start=True, stop=True))
            bank_w[7] = t_gs
            sy.wait("S", t_gs)
            sch(S.activation, ginv2, ps[7][0:1, 0:512], AF.Ln)
            t_gi = sch(S.activation, ginv2, ginv2, AF.Exp, scale=-1.0)
            bank_r[7] = t_gi
            sy.waits("T", [bank_r[7], t_gi])
            t_b4 = sy.inc("T", T.matmul(ps[7][0:4, 0:512], onesr[0:1, 0:4], ginv2, start=True, stop=True))
            bank_w[7] = t_b4
            sy.waits("V", [t_b4, t_ge])
            t_g4 = vch(V.tensor_mul, gates4, geexp, ps[7][0:4, 0:512])
            bank_r[7] = t_g4
            sy.waits("T", [bank_r[7], t_g4, tk_e4])
            t_ex = sy.inc("T", T.matmul(ps[7][0:32, 0:512], exp4sb[:, :], gates4, start=True, stop=True))
            bank_w[7] = t_ex
            sy.waits("V", [t_ex, e_pj])
            t_sp = vch(V.tensor_mul, sprojb, projf, ps[7][0:32, 0:512])
            bank_r[7] = t_sp
            if ln2_pend is not None:
                ln2_norm(ln2_pend, t_sc2)
                ln2_pend = None

            for os in range(8):
                par = os % 2
                tk_w = bank_w.pop(("wdma", os), None)
                for tb in range(4):
                    bi = (os * 4 + tb) % 8
                    sy.waits("T", [bank_r.get(bi), tk_w, t_x1, tk_setup, tk_bmt])
                    if not (os == 0 and tb == 0):
                        for it in range(NCH):
                            T.matmul(ps[bi][0:128, 0:512],
                                     r1b[:, it * TT + tb * 128: it * TT + (tb + 1) * 128],
                                     wtb[par][:, it * 512:(it + 1) * 512],
                                     start=(it == 0), stop=False)
                    T.matmul(ps[bi][0:128, 0:512], onesrb[0:1, :],
                             biasrb2[0:1, os * 512:(os + 1) * 512], start=False, stop=False)
                    sy.waits("T", [t_sp])
                    lm = T.matmul(ps[bi][0:128, 0:512],
                                  sprojb[:, tb * 128:(tb + 1) * 128],
                                  bmtb[:, os * 512:(os + 1) * 512], start=False, stop=True)
                    bank_w[bi] = sy.inc("T", lm)
                    dst = scr16[:, tb * D4 + os * 512: tb * D4 + (os + 1) * 512]
                    if (os + tb) % 2 == 0:
                        sy.waits("V", [bank_w[bi], bank_r.get("scr16")])
                        ev = vch(V.tensor_copy, dst, ps[bi][0:128, 0:512])
                    else:
                        sy.waits("S", [bank_w[bi], bank_r.get("scr16")])
                        ev = sch(S.copy, dst, ps[bi][0:128, 0:512])
                    bank_r[bi] = ev
                bank_w[("wtb", par)] = sy.tok("T")
                if t == 0:
                    sy.waits("Y", [tk_w])
                    bank_w[("wexp", par)] = ydma(f"YW{par}", wscr.ap()[os, :], wtb[par][:, :])
                if os + 2 < 8:
                    w_dma(os + 2)
                if os == 4 and t + 1 < ntile:
                    uq_tok[0] = u_dma(t + 1, 0, [t_conv, t_convT, t_convS])
                if os == 5 and t + 1 < ntile:
                    uq_tok[1] = u_dma(t + 1, 1, [])
            ln2_pend = (t, sy.tok("V"), sy.tok("S"))

        t_vc2 = ln2_stats(ln2_pend)
        t_sc2 = ln2_scal(t_vc2)
        ln2_norm(ln2_pend, t_sc2)

        for k, semh in sy.sem.items():
            if sy.cnt[k]:
                eng = sy.eng["Y"] if k[0] in "YG" else sy.eng[k]
                eng.wait_ge(semh, sy.cnt[k])
    return nc


def _host_prep(inputs, ntile):
    ntok = ntile * TT
    f32 = np.float32
    u = np.ascontiguousarray(inputs["L0_attn_u"], dtype=f32).reshape(-1, D4)
    WTh = np.ascontiguousarray(np.asarray(inputs["L0_ffn_W"], dtype=f32).T)
    CatAh = np.ascontiguousarray(np.concatenate([
        np.asarray(inputs["L0_ffn_A"], dtype=f32).transpose(2, 0, 1).reshape(D4, 32),
        np.asarray(inputs["L0_ffn_Wr"], dtype=f32).T], axis=1))
    BmTh = np.ascontiguousarray(
        np.asarray(inputs["L0_ffn_Bm"], dtype=f32).transpose(0, 2, 1).reshape(32, D4))
    biash = np.ascontiguousarray(np.asarray(inputs["L0_ffn_b"], dtype=f32))
    lam = np.asarray(inputs["L0_attn_Lambda_ri"], dtype=f32)
    pv = np.asarray(inputs["L0_attn_p_ri"], dtype=f32)
    qv = np.asarray(inputs["L0_attn_q_ri"], dtype=f32)
    ctv = np.asarray(inputs["L0_attn_Ct_ri"], dtype=f32).reshape(64, 2)
    vc = np.asarray(inputs["L0_attn_Vc_ri"], dtype=f32)
    smalls = {
        "lam_re": lam[:, 0], "lam_im": lam[:, 1],
        "p_re": pv[:, 0], "p_im": pv[:, 1],
        "q_re": qv[:, 0], "q_im": qv[:, 1],
        "ct_re": ctv[:, 0], "ct_im": ctv[:, 1],
        "Bpv": np.asarray(inputs["L0_attn_Bp"], dtype=f32).reshape(64),
    }
    smalls = {k: np.ascontiguousarray(np.concatenate([v, v])) for k, v in smalls.items()}
    VcT_re = np.ascontiguousarray(vc[..., 0].T)
    VcT_im = np.ascontiguousarray(vc[..., 1].T)
    Dh = np.ascontiguousarray(np.asarray(inputs["L0_attn_D"], dtype=f32).reshape(1))
    lsh = np.ascontiguousarray(np.asarray(inputs["L0_attn_log_step"], dtype=f32).reshape(1))

    k_ = np.arange(D4)
    Om = np.exp(-2j * np.pi * (k_ / D4))
    cval = (1.0 - Om) / (1.0 + Om)
    cc = 2.0 / (1.0 + Om)
    cvr = cval.real.astype(f32).reshape(2, 2048)
    cvi = cval.imag.astype(f32).reshape(2, 2048)
    cvb_re = np.ascontiguousarray(np.repeat(cvr, 64, axis=0))
    cvb_im = np.ascontiguousarray(np.repeat(cvi, 64, axis=0))
    cc_re = np.ascontiguousarray(cc.real.astype(f32).reshape(128, 32))
    cc_im = np.ascontiguousarray(cc.imag.astype(f32).reshape(128, 32))
    F = np.exp(2j * np.pi * np.outer(np.arange(64), np.arange(64)) / 64.0)
    Tw = np.exp(2j * np.pi * np.outer(np.arange(64), np.arange(64)) / D4)
    ftab = np.ascontiguousarray(np.concatenate([
        F.real, F.imag, -F.imag, F.real / D4, -F.imag / D4, Tw.real, Tw.imag,
    ], axis=1).astype(f32))
    J128 = np.ascontiguousarray(np.eye(128, dtype=f32)[::-1])
    eye128 = np.eye(128, dtype=f32)
    ones_row = np.ones((1, 128), dtype=f32)
    ones_col = np.ones((128, 1), dtype=f32)
    EXP4 = np.zeros((4, 32), dtype=f32)
    for e in range(4):
        EXP4[e, e * 8:(e + 1) * 8] = 2.0

    common = dict(WT=WTh, CatA=CatAh, BmTd=BmTh, biasd=biash,
                  VcT_re=VcT_re, VcT_im=VcT_im, Dv=Dh, lsd=lsh,
                  cvb_re=cvb_re, cvb_im=cvb_im, cc_re=cc_re, cc_im=cc_im,
                  ftab=ftab, J128=J128, eye128=eye128,
                  ones_row=ones_row, ones_col=ones_col, EXP4=EXP4, **smalls)
    in_maps = []
    for c in range(NCORES):
        m = dict(common)
        m["uT"] = np.ascontiguousarray(u[c * ntok:(c + 1) * ntok].T)
        in_maps.append(m)
    return in_maps


def kernel(**inputs):
    ntile = 2048 // TT
    in_maps = _host_prep(inputs, ntile)
    nc = build_nc(ntile)
    res = run_bass_kernel_spmd(nc, in_maps, core_ids=list(range(NCORES)))
    out = np.concatenate([res.results[i]["o"] for i in range(NCORES)], axis=0)
    return out.reshape(2, 8192, D4)


# version 28 (final)
# speedup vs baseline: 1.0453x; 1.0453x over previous
import contextlib
import numpy as np
import concourse.bass as bass
import concourse.mybir as mybir
from concourse.bass import AP
from concourse.bass_utils import run_bass_kernel_spmd
from concourse.alu_op_type import AluOpType as OP

F32 = mybir.dt.float32
BF16 = mybir.dt.bfloat16
AF = mybir.ActivationFunctionType

D4 = 4096
NCH = 32
TT = 512
NCORES = 8
EPS = 1e-5
LOG2 = 0.6931471805599453


class Sync:

    def __init__(self, nc, st):
        self.nc = nc
        self.st = st
        self.eng = {"T": nc.tensor, "V": nc.vector, "S": nc.scalar,
                    "G": nc.gpsimd, "Y": nc.sync}
        self.sem = {k: st.enter_context(nc.semaphore(f"sem{k}")) for k in self.eng}
        self.cnt = {k: 0 for k in self.eng}
        self.last = {}

    def new_sem(self, key):
        self.sem[key] = self.st.enter_context(self.nc.semaphore(f"sem{key}"))
        self.cnt[key] = 0

    def inc(self, k, instr, n=1):
        instr.then_inc(self.sem[k], n)
        self.cnt[k] += n
        return (k, self.cnt[k])

    def tok(self, k):
        return (k, self.cnt[k])

    def wait(self, consumer, tok):
        if tok is None:
            return
        k, v = tok
        if v <= 0:
            return
        key = (consumer, k)
        if self.last.get(key, 0) >= v:
            return
        self.eng[consumer].wait_ge(self.sem[k], v)
        self.last[key] = v

    def waits(self, consumer, toks):
        best = {}
        for t in toks:
            if t is None:
                continue
            k, v = t
            if v > best.get(k, 0):
                best[k] = v
        for k, v in best.items():
            self.wait(consumer, (k, v))


def build_nc(ntile):
    ntok = ntile * TT
    nc = bass.Bass("TRN2", target_bir_lowering=False)

    def dti(name, shp, dtype=F32, kind="ExternalInput"):
        return nc.dram_tensor(name, shp, dtype, kind=kind)

    uT = dti("uT", [D4, ntok])
    WT = dti("WT", [D4, D4])
    CatA = dti("CatA", [D4, 36])
    BmTd = dti("BmTd", [32, D4])
    biasd = dti("biasd", [D4])
    smd = {n: dti(n, [128]) for n in ["lam_re", "lam_im", "p_re", "p_im", "q_re",
                                      "q_im", "ct_re", "ct_im", "Bpv"]}
    vctre_d = dti("VcT_re", [64, 64]); vctim_d = dti("VcT_im", [64, 64])
    Dv = dti("Dv", [1]); lsd = dti("lsd", [1])
    cvbre_d = dti("cvb_re", [128, 2048]); cvbim_d = dti("cvb_im", [128, 2048])
    ccre_d = dti("cc_re", [128, 32]); ccim_d = dti("cc_im", [128, 32])
    ftd = dti("ftab", [64, 448])
    jd = dti("J128", [128, 128]); eyed = dti("eye128", [128, 128])
    onesrd = dti("ones_row", [1, 128]); onescd = dti("ones_col", [128, 1])
    exp4d = dti("EXP4", [4, 32])
    out_d = dti("o", [ntok, D4], F32, kind="ExternalOutput")
    scr = dti("scr", [20, D4], F32, kind="Internal")
    wscr = dti("wscr", [8, 128 * NCH * 512], BF16, kind="Internal")
    kpad = dti("kpad", [4224], F32, kind="Internal")

    with contextlib.ExitStack() as st:
        sy = Sync(nc, st)
        T, V, S, G, Y = (sy.eng[k] for k in "TVSGY")

        def sb(name, shp, dtype):
            return st.enter_context(nc.sbuf_tensor(name, shp, dtype))

        ps = [st.enter_context(nc.psum_tensor(f"ps{i}", [128, TT], F32)) for i in range(8)]

        def vch(fn, *a, **kw):
            t = sy.inc("V", fn(*a, **kw))
            sy.wait("V", t)
            return t

        def sch(fn, *a, **kw):
            t = sy.inc("S", fn(*a, **kw))
            sy.wait("S", t)
            return t

        for k in ["GS", "GP", "GG", "GU0", "GU1", "GW0", "GW1", "GH0", "GH1",
                  "YB", "YS0", "YS1", "YO0", "YO1", "YW0", "YW1"]:
            sy.new_sem(k)

        def gdma(key, out_ap, in_ap):
            return sy.inc(key, G.dma_start(out=out_ap, in_=in_ap), 16)

        def ydma(key, out_ap, in_ap):
            return sy.inc(key, Y.dma_start(out=out_ap, in_=in_ap), 16)

        uTb = sb("uTb", [128, 16 * TT], BF16)
        onesr = sb("onesr", [1, 128], F32)
        onesc = sb("onesc", [128, 1], F32)
        onescb = sb("onescb", [128, 1], BF16)
        smallcol = sb("smallcol", [128, 16], F32)
        dls = sb("dls", [1, 4], F32)
        sc1 = sb("sc1", [1, 4], F32)
        opdcol = sb("opdcol", [128, 1], F32)
        ztile = sb("ztile", [1, 128], F32)
        Vstack = sb("Vstack", [64, 8], F32)
        tc = sb("tc", [64, 2], F32)
        vct_re = sb("vct_re", [64, 64], F32); vct_im = sb("vct_im", [64, 64], F32)
        ccre = sb("ccre", [128, 32], F32); ccim = sb("ccim", [128, 32], F32)
        ft = sb("ft", [64, 448], F32)
        biasrb2 = sb("biasrb2", [1, D4], BF16)
        onesrb = sb("onesrb", [1, 128], BF16)
        for i, n in enumerate(["lam_re", "lam_im", "p_re", "p_im", "q_re",
                               "q_im", "ct_re", "ct_im", "Bpv"]):
            gdma("GS", smallcol[:, i:i + 1], smd[n].ap())
        gdma("GS", dls[0:1, 0:1], Dv.ap())
        gdma("GS", dls[0:1, 1:2], lsd.ap())
        gdma("GS", vct_re[:, :], vctre_d.ap())
        gdma("GS", vct_im[:, :], vctim_d.ap())
        gdma("GS", ccre[:, :], ccre_d.ap())
        gdma("GS", ccim[:, :], ccim_d.ap())
        gdma("GS", ft[:, :], ftd.ap())
        gdma("GS", onesr[0:1, :], onesrd.ap())
        gdma("GS", onesc[:, :], onescd.ap())
        gdma("GS", onescb[:, :], onescd.ap())
        tk_bb = gdma("GS", biasrb2[0:1, :], biasd.ap())
        gdma("GS", onesrb[0:1, :], onesrd.ap())
        t_bB = tk_bb

        tk_setup = sy.tok("GS")
        tk_small = [tk_setup] * 9
        tk_D = tk_ls = tk_vct = tk_ft = tk_or = tk_oc = tk_ocb = tk_setup
        tk_cc = [tk_setup]

        def u_dma(t, q, war_toks):
            sy.waits("G", war_toks)
            return gdma(f"GU{q % 2}", uTb[:, 8 * (q % 2) * TT: (8 * (q % 2) + 8) * TT],
                        AP(uT.ap().tensor, 8 * q * 128 * ntok + t * TT,
                           [[ntok, 128], [128 * ntok, 8], [1, TT]]))

        sy.waits("S", [tk_D, tk_ls])
        sch(S.activation, sc1[0:1, 0:1], dls[0:1, 1:2], AF.Exp, scale=-1.0)
        t_twod = sch(S.mul, sc1[0:1, 1:2], sc1[0:1, 0:1], 2.0)
        t_opd = sch(S.add, sc1[0:1, 2:3], dls[0:1, 0:1], 1.0)


        sy.waits("T", [tk_or, t_twod, t_opd])
        g1t = sy.inc("T", T.matmul(ps[0][0:128, 0:1], onesr[0:1, 0:128], sc1[0:1, 1:2], start=True, stop=True))
        g2t = sy.inc("T", T.matmul(ps[0][0:128, 1:2], onesr[0:1, 0:128], sc1[0:1, 2:3], start=True, stop=True))
        sy.waits("V", [g1t, g2t])
        t_twocol = vch(V.tensor_copy, smallcol[:, 9:10], ps[0][0:128, 0:1])
        t_opdcol = vch(V.tensor_copy, opdcol[:, :], ps[0][0:128, 1:2])

        sy.waits("T", [tk_vct, tk_small[8], t_twocol, t_opdcol])
        bt1 = sy.inc("T", T.matmul(ps[0][0:64, 0:1], vct_re[:, :], smallcol[0:64, 8:9], start=True, stop=True))
        bt2 = sy.inc("T", T.matmul(ps[0][0:64, 1:2], vct_im[:, :], smallcol[0:64, 8:9], start=True, stop=True))
        sy.waits("V", [bt1, bt2])
        vch(V.tensor_copy, smallcol[0:64, 10:12], ps[0][0:64, 0:2])

        sy.waits("V", [tk_small[7], tk_small[5]])
        vch(V.tensor_scalar_mul, smallcol[0:64, 12:13], smallcol[0:64, 7:8], -1.0)
        vch(V.tensor_scalar_mul, smallcol[0:64, 13:14], smallcol[0:64, 5:6], -1.0)
        sy.waits("V", tk_small[:7])

        BRE, BIM = smallcol[0:64, 10:11], smallcol[0:64, 11:12]
        PRE, PIM = smallcol[0:64, 2:3], smallcol[0:64, 3:4]
        CTR, CTI = smallcol[0:64, 6:7], smallcol[0:64, 12:13]
        QR, QI = smallcol[0:64, 4:5], smallcol[0:64, 13:14]

        def cmul_into(dre, dim_, ar, ai, br, bi):
            vch(V.tensor_mul, tc[:, 0:1], ar, br)
            vch(V.tensor_mul, tc[:, 1:2], ai, bi)
            vch(V.tensor_sub, dre, tc[:, 0:1], tc[:, 1:2])
            vch(V.tensor_mul, tc[:, 0:1], ar, bi)
            vch(V.tensor_mul, tc[:, 1:2], ai, br)
            vch(V.tensor_add, dim_, tc[:, 0:1], tc[:, 1:2])

        cmul_into(Vstack[:, 0:1], Vstack[:, 1:2], CTR, CTI, BRE, BIM)
        cmul_into(Vstack[:, 2:3], Vstack[:, 3:4], CTR, CTI, PRE, PIM)
        cmul_into(Vstack[:, 4:5], Vstack[:, 5:6], QR, QI, BRE, BIM)
        cmul_into(Vstack[:, 6:7], Vstack[:, 7:8], QR, QI, PRE, PIM)
        t_vstack = sy.tok("V")

        win_sb = sb("win_sb", [128, 256], BF16)
        wout_sb = sb("wout_sb", [128, 256], BF16)
        pstat = sb("pstat", [128, 128], BF16)
        band0b = sb("band0b", [128, 128], BF16)
        band1b = sb("band1b", [128, 128], BF16)
        eyeop = sb("eyeop", [128, 128], BF16)
        carry = [sb(f"carry{i}", [128, TT], BF16) for i in range(2)]
        jj = sb("jj", [128, 128], F32)
        hankb = [sb(f"hank{i}", [128, 128], F32) for i in range(2)]
        sy.waits("G", [sy.tok("V"), sy.tok("S")])
        tk_j = gdma("GH0", jj[:, :], jd.ap())

        tk_u0 = None
        tk_u0b = None
        t_band0 = None
        with contextlib.ExitStack() as st2:
            def sb2(name, shp, dtype=F32):
                return st2.enter_context(nc.sbuf_tensor(name, shp, dtype))

            def kpath():
                nonlocal tk_u0, tk_u0b, t_band0
                g1 = sb2("g1", [128, 2048], F32)
                g2 = sb2("g2", [128, 2048], F32)
                g3 = sb2("g3", [128, 2048], F32)
                g4 = sb2("g4", [128, 2048], F32)
                lncol = sb2("lncol", [128, 1], F32)
                vs128 = sb2("vs128", [128, 16], F32)
                stage8 = [sb2(f"stage8{i}", [16, 512], F32) for i in range(2)]
                stage_tok = [None, None]
                t_vz = vch(V.memset, vs128[:, :], 0.0)
                sy.waits("Y", [t_vstack, t_vz])
                yv1 = ydma("YB", AP(scr.ap().tensor, 0, [[1, 512]]), Vstack[:, :])
                sy.waits("Y", [yv1])
                ydma("YB", vs128[0:64, 0:8], AP(scr.ap().tensor, 0, [[8, 64], [1, 8]]))
                yv3 = ydma("YB", vs128[64:128, 8:16], AP(scr.ap().tensor, 0, [[8, 64], [1, 8]]))
                sy.waits("G", [sy.tok("T")])
                tg1 = gdma("GG", g1[:, :], cvbre_d.ap())
                tg2 = gdma("GG", g2[:, :], cvbim_d.ap())
                tk_u0 = u_dma(0, 0, [])
                tk_u0b = u_dma(0, 1, [])
                yield
                sy.waits("V", [tg1, tg2, t_twocol, tk_small[0], tk_small[1]])
                vch(V.tensor_scalar_mul, g1[:, :], g1[:, :], smallcol[:, 9:10])
                vch(V.tensor_scalar_sub, g1[:, :], g1[:, :], smallcol[:, 0:1])
                vch(V.tensor_scalar_mul, g2[:, :], g2[:, :], smallcol[:, 9:10])
                v4 = vch(V.tensor_scalar_sub, g2[:, :], g2[:, :], smallcol[:, 1:2])
                yield
                t_lnc = vch(V.memset, lncol[:, :], -55.451774444795625)
                sy.waits("S", [v4, t_lnc])
                sq1 = sch(S.activation, g3[:, :], g1[:, :], AF.Square, scale=2.0 ** -40)
                sq2 = sch(S.activation, g4[:, :], g2[:, :], AF.Square, scale=2.0 ** -40)
                yield
                sy.waits("V", [sq1, sq2])
                v5 = vch(V.tensor_add, g3[:, :], g3[:, :], g4[:, :])
                sy.wait("S", v5)
                sch(S.activation, g3[:, :], g3[:, :], AF.Ln)
                s4t = sch(S.activation, g3[:, :], g3[:, :], AF.Exp, scale=-1.0, bias=lncol[:, :])
                s5t = sch(S.mul, g2[:, :], g2[:, :], -1.0)
                yield
                sy.waits("V", [s4t, s5t])
                v6 = vch(V.tensor_mul, g1[:, :], g1[:, :], g3[:, :])
                v7 = vch(V.tensor_mul, g2[:, :], g2[:, :], g3[:, :])
                yield
                sy.waits("T", [v6, v7, yv3, t_bB])
                gbank_t = {}
                for c in range(4):
                    csl = slice(c * 512, (c + 1) * 512)
                    bank = ps[6 + c % 2]
                    for half, iv in ((0, g1), (1, g2)):
                        sg = half
                        sy.waits("T", [gbank_t.get(id(bank))])
                        m1 = sy.inc("T", T.matmul(bank[0:16, 0:512], vs128[:, :], iv[:, csl],
                                                  start=True, stop=True))
                        sy.waits("V", [m1, stage_tok[sg]])
                        sy.waits("Y", [yv3])
                        e1 = vch(V.tensor_copy, stage8[sg][:, :], bank[0:16, 0:512])
                        gbank_t[id(bank)] = e1
                        sy.wait("Y", e1)
                        base = (8 * D4 if half == 1 else 0) + c * 512
                        dtok = ydma(f"YS{sg}", AP(scr.ap().tensor, base, [[2048, 2], [D4, 8], [1, 512]]),
                                    stage8[sg][:, :])
                        stage_tok[sg] = dtok
                    yield

                kk = sb2("kk", [128, 512], F32)
                sy.waits("Y", stage_tok)
                b3 = ydma("YB", kk[:, :], AP(scr.ap().tensor, 0, [[32, 128], [D4, 16], [1, 32]]))

                ta = sb2("ta", [128, 8 * 32], F32)
                tb_ = sb2("tb_", [128, 6 * 32], F32)
                kb = lambda r: kk[:, r * 32:(r + 1) * 32]
                ab = lambda r: ta[:, r * 32:(r + 1) * 32]
                tbb = lambda r: tb_[:, r * 32:(r + 1) * 32]
                sy.wait("V", b3)
                yield
                vch(V.tensor_sub, ab(0), kb(0), kb(9));   vch(V.tensor_add, ab(1), kb(1), kb(8))
                vch(V.tensor_sub, ab(2), kb(2), kb(11));  vch(V.tensor_add, ab(3), kb(3), kb(10))
                vch(V.tensor_sub, ab(4), kb(4), kb(13));  vch(V.tensor_add, ab(5), kb(5), kb(12))
                vch(V.tensor_sub, ab(6), kb(6), kb(15));  t_ab7 = vch(V.tensor_add, ab(7), kb(7), kb(14))
                t_ab6 = vch(V.tensor_scalar_add, ab(6), ab(6), 1.0)
                yield
                sy.waits("S", [t_ab6, t_ab7])
                w1 = sch(S.square, tbb(0), ab(6))
                w2 = sch(S.square, tbb(1), ab(7))
                sy.waits("V", [w1, w2])
                w3 = vch(V.tensor_add, tbb(0), tbb(0), tbb(1))
                sy.wait("S", w3)
                sch(S.activation, tbb(0), tbb(0), AF.Ln)
                w4 = sch(S.activation, tbb(0), tbb(0), AF.Exp, scale=-1.0)
                w5 = sch(S.mul, ab(7), ab(7), -1.0)
                yield
                sy.waits("V", [w4, w5])
                vch(V.tensor_mul, ab(6), ab(6), tbb(0))
                vch(V.tensor_mul, ab(7), ab(7), tbb(0))
                vch(V.tensor_mul, tbb(1), ab(2), ab(4))
                vch(V.tensor_mul, tbb(2), ab(3), ab(5))
                vch(V.tensor_sub, tbb(1), tbb(1), tbb(2))
                vch(V.tensor_mul, tbb(2), ab(2), ab(5))
                vch(V.tensor_mul, tbb(3), ab(3), ab(4))
                vch(V.tensor_add, tbb(2), tbb(2), tbb(3))
                yield
                vch(V.tensor_mul, tbb(3), tbb(1), ab(6))
                vch(V.tensor_mul, tbb(4), tbb(2), ab(7))
                vch(V.tensor_sub, tbb(3), tbb(3), tbb(4))
                vch(V.tensor_mul, tbb(4), tbb(1), ab(7))
                vch(V.tensor_mul, tbb(5), tbb(2), ab(6))
                vch(V.tensor_add, tbb(4), tbb(4), tbb(5))
                vch(V.tensor_sub, ab(0), ab(0), tbb(3))
                vch(V.tensor_sub, ab(1), ab(1), tbb(4))
                yield
                sy.waits("V", tk_cc)
                vch(V.tensor_mul, tbb(0), ccre[:, :], ab(0))
                vch(V.tensor_mul, tbb(1), ccim[:, :], ab(1))
                vch(V.tensor_sub, tbb(0), tbb(0), tbb(1))
                vch(V.tensor_mul, tbb(1), ccre[:, :], ab(1))
                vch(V.tensor_mul, tbb(2), ccim[:, :], ab(0))
                t_ar = vch(V.tensor_add, tbb(1), tbb(1), tbb(2))
                yield

                sy.wait("Y", t_ar)
                b4 = ydma("YB", AP(scr.ap().tensor, 16 * D4, [[1, D4]]), tbb(0))
                b5 = ydma("YB", AP(scr.ap().tensor, 17 * D4, [[1, D4]]), tbb(1))
                Rre = sb2("Rre", [64, 64], F32)
                Rim = sb2("Rim", [64, 64], F32)
                sy.waits("Y", [b4, b5])
                b6 = ydma("YB", Rre[:, :], AP(scr.ap().tensor, 16 * D4, [[64, 64], [1, 64]]))
                b7 = ydma("YB", Rim[:, :], AP(scr.ap().tensor, 17 * D4, [[64, 64], [1, 64]]))
                yield

                FRE, FIM, FIMN = ft[:, 0:64], ft[:, 64:128], ft[:, 128:192]
                FCRE, FCIMN = ft[:, 192:256], ft[:, 256:320]
                TRE, TIM = ft[:, 320:384], ft[:, 384:448]
                sy.waits("T", [b6, b7, tk_ft, gbank_t.get(id(ps[6]))])
                sy.inc("T", T.matmul(ps[6][0:64, 0:64], Rre[:, :], FRE, start=True, stop=False))
                am1 = sy.inc("T", T.matmul(ps[6][0:64, 0:64], Rim[:, :], FIMN, start=False, stop=True))
                sy.inc("T", T.matmul(ps[6][0:64, 64:128], Rre[:, :], FIM, start=True, stop=False))
                am2 = sy.inc("T", T.matmul(ps[6][0:64, 64:128], Rim[:, :], FRE, start=False, stop=True))
                a1sb = sb2("a1sb", [64, 128], F32)
                a2sb = sb2("a2sb", [64, 128], F32)
                t64a = sb2("t64a", [64, 64], F32)
                t64b = sb2("t64b", [64, 64], F32)
                sy.waits("V", [am1, am2])
                vch(V.tensor_copy, a1sb[:, :], ps[6][0:64, 0:128])
                vch(V.tensor_mul, t64a[:, :], a1sb[:, 0:64], TRE)
                vch(V.tensor_mul, t64b[:, :], a1sb[:, 64:128], TIM)
                vch(V.tensor_sub, a2sb[:, 0:64], t64a[:, :], t64b[:, :])
                vch(V.tensor_mul, t64a[:, :], a1sb[:, 0:64], TIM)
                vch(V.tensor_mul, t64b[:, :], a1sb[:, 64:128], TRE)
                t_a2 = vch(V.tensor_add, a2sb[:, 64:128], t64a[:, :], t64b[:, :])
                yield
                sy.wait("T", t_a2)
                sy.inc("T", T.matmul(ps[7][0:64, 0:64], FCRE, a2sb[:, 0:64], start=True, stop=False))
                kmt = sy.inc("T", T.matmul(ps[7][0:64, 0:64], FCIMN, a2sb[:, 64:128], start=False, stop=True))
                Ksb = sb2("Ksb", [64, 64], F32)
                sy.wait("V", kmt)
                t_ksb = vch(V.tensor_copy, Ksb[:, :], ps[7][0:64, 0:64])
                yield

                t_z = vch(V.memset, ztile[0:1, :], 0.0)
                sy.waits("Y", [t_z, t_ksb])
                z1 = ydma("YB", AP(kpad.ap().tensor, 0, [[1, 127]]), ztile[0:1, 0:127])
                z2 = ydma("YB", AP(kpad.ap().tensor, 127, [[1, D4]]), Ksb[:, :])
                z3 = ydma("YB", AP(kpad.ap().tensor, 4223, [[1, 1]]), ztile[0:1, 0:1])
                tk_kpad = [z1, z2, z3]
                yield

                sy.waits("G", tk_kpad)
                hg = gdma("GH0", hankb[0][:, :], AP(kpad.ap().tensor, 0, [[1, 128], [1, 128]]))
                hg1 = gdma("GH0", hankb[1][:, :], AP(kpad.ap().tensor, 128, [[1, 128], [1, 128]]))
                hg = hg1
                sy.waits("T", [hg, tk_j])
                bm = sy.inc("T", T.matmul(ps[4][0:128, 0:128], jj[:, :], hankb[0][:, :], start=True, stop=True))
                sy.wait("S", bm)
                t_band0 = sch(S.copy, band0b[:, :], ps[4][0:128, 0:128])
                sy.waits("T", [hg1])
                bm1 = sy.inc("T", T.matmul(ps[5][0:128, 0:128], jj[:, :], hankb[1][:, :], start=True, stop=True))
                sy.wait("S", bm1)
                t_band0 = sch(S.copy, band1b[:, :], ps[5][0:128, 0:128])

                yield

            kg = kpath()
            kdone = [False]

            def kstep(n=1):
                for _ in range(n):
                    if kdone[0]:
                        return
                    if next(kg, "END") == "END":
                        kdone[0] = True

            kstep(2)
            col2 = sb2("col2", [64, 32])
            srow = sb2("srow", [1, 16])
            rows8 = sb2("rows8", [1, 768])
            eye64f = sb2("eye64f", [64, 64])
            osb = sb2("osb", [64, 128])
            dgm = sb2("dgm", [64, 64])
            t64c = sb2("t64c", [64, 64])
            t64d = sb2("t64d", [64, 64])
            ycol = sb2("ycol", [64, 512])
            crow = sb2("crow", [64, 512])
            ytr = sb2("ytr", [128, 64])
            mats = {}

            def cm(name):
                mats[name] = sb2(name, [64, 192])
                return mats[name]

            def mre(n): return mats[n][:, 0:64]
            def mim(n): return mats[n][:, 64:128]
            def mimN(n): return mats[n][:, 128:192]

            sy.waits("G", [sy.tok("V"), sy.tok("S")])
            ge64 = gdma("GH0", eye64f[:, :], eyed.ap()[0:64, 0:64])

            C = lambda i: col2[:, i:i + 1]
            SM = lambda i: smallcol[0:64, i:i + 1]
            SR = lambda i: srow[0:1, i:i + 1]
            t_dl = sch(S.activation, SR(0), dls[0:1, 1:2], AF.Exp)
            t_hd = sch(S.mul, SR(1), SR(0), 0.5)
            sy.waits("T", [t_hd, tk_or, sy.tok("V")])
            hm = sy.inc("T", T.matmul(ps[0][0:64, 0:1], onesr[0:1, 0:64], SR(1), start=True, stop=True))
            sy.wait("V", hm)
            vch(V.tensor_copy, C(0), ps[0][0:64, 0:1])
            vch(V.tensor_scalar_mul, C(1), SM(0), C(0))
            vch(V.tensor_scalar_mul, C(1), C(1), -1.0)
            vch(V.tensor_scalar_add, C(1), C(1), 1.0)
            vch(V.tensor_scalar_mul, C(2), SM(1), C(0))
            vch(V.tensor_scalar_mul, C(2), C(2), -1.0)
            vch(V.tensor_mul, C(3), C(1), C(1))
            vch(V.tensor_mul, C(4), C(2), C(2))
            vch(V.tensor_add, C(3), C(3), C(4))
            vch(V.reciprocal, C(4), C(3))
            vch(V.tensor_mul, C(5), C(1), C(4))
            vch(V.tensor_mul, C(6), C(2), C(4))
            vch(V.tensor_scalar_mul, C(6), C(6), -1.0)

            def cmul_cols(dr, di, ar, ai, br, bi):
                vch(V.tensor_mul, C(14), ar, br)
                vch(V.tensor_mul, C(15), ai, bi)
                vch(V.tensor_sub, dr, C(14), C(15))
                vch(V.tensor_mul, C(14), ar, bi)
                vch(V.tensor_mul, C(15), ai, br)
                vch(V.tensor_add, di, C(14), C(15))

            cmul_cols(C(7), C(8), C(5), C(6), SM(2), SM(3))
            cmul_cols(C(9), C(10), SM(4), SM(13), C(5), C(6))
            cmul_cols(C(11), C(12), C(7), C(8), SM(4), SM(13))
            t_w = sy.tok("V")
            sy.waits("T", [t_w, tk_oc])
            sm0 = sy.inc("T", T.matmul(ps[0][0:1, 0:2], onesc[0:64, 0:1], col2[:, 11:13], start=True, stop=True))
            sy.wait("V", sm0)
            vch(V.tensor_copy, srow[0:1, 2:4], ps[0][0:1, 0:2])
            vch(V.tensor_mul, SR(4), SR(1), SR(2))
            vch(V.tensor_scalar_add, SR(4), SR(4), 1.0)
            vch(V.tensor_mul, SR(5), SR(1), SR(3))
            vch(V.tensor_mul, SR(6), SR(4), SR(4))
            vch(V.tensor_mul, SR(7), SR(5), SR(5))
            vch(V.tensor_add, SR(6), SR(6), SR(7))
            vch(V.reciprocal, SR(7), SR(6))
            vch(V.tensor_mul, SR(8), SR(4), SR(7))
            vch(V.tensor_mul, SR(8), SR(8), SR(1))
            vch(V.tensor_mul, SR(9), SR(5), SR(7))
            vch(V.tensor_mul, SR(9), SR(9), SR(1))
            t_s = vch(V.tensor_scalar_mul, SR(9), SR(9), -1.0)
            sy.waits("T", [t_s])
            sb1 = sy.inc("T", T.matmul(ps[0][0:64, 2:4], onesr[0:1, 0:64], srow[0:1, 8:10], start=True, stop=True))
            sy.wait("V", sb1)
            vch(V.tensor_copy, col2[:, 16:18], ps[0][0:64, 2:4])
            vch(V.tensor_scalar_mul, C(18), C(8), -1.0)
            vch(V.tensor_scalar_mul, C(19), C(10), -1.0)
            vch(V.tensor_scalar_mul, C(20), SM(3), -1.0)
            t_cols = vch(V.tensor_scalar_mul, C(21), SM(13), -1.0)
            sy.wait("Y", t_cols)
            yb = None
            rowsrc = [C(7), C(8), C(9), C(10), SM(2), SM(3), SM(4), SM(13),
                      C(18), C(19), C(20), C(21)]
            sy.waits("Y", [sy.tok("YB")])
            for i, src in enumerate(rowsrc):
                yb = ydma("YB", AP(scr.ap().tensor, 18 * D4 + 64 * i, [[1, 64]]), src)
            sy.waits("Y", [yb])
            yb2 = ydma("YB", rows8[0:1, :], AP(scr.ap().tensor, 18 * D4, [[1, 768]]))
            R = lambda i: rows8[0:1, i * 64:(i + 1) * 64]
            ARE, AIM, BRE2, BIM2 = R(0), R(1), R(2), R(3)
            PRE2, PIM2, QBR, QBI = R(4), R(5), R(6), R(7)
            AIMN, BIMN, PIMN, QBIN = R(8), R(9), R(10), R(11)

            bank_t = {}
            mat_t = {}

            def outer_pair(bank, ur, uim, uimN, vr, vi, extra=()):
                sy.waits("T", [yb2, *extra, bank_t.get(id(bank))])
                T.matmul(bank[0:64, 0:64], ur, vr, start=True, stop=False)
                T.matmul(bank[0:64, 0:64], uimN, vi, start=False, stop=True)
                T.matmul(bank[0:64, 64:128], ur, vi, start=True, stop=False)
                mm4 = sy.inc("T", T.matmul(bank[0:64, 64:128], uim, vr, start=False, stop=True))
                return mm4

            def build_M(name, dcol_re, dcol_im, scol_re_ap, scol_im_ap,
                        ur, uim, uimN, vr, vi):
                mm4 = outer_pair(ps[1], ur, uim, uimN, vr, vi)
                sy.wait("V", mm4)
                ev = vch(V.tensor_copy, osb[:, :], ps[1][0:64, 0:128])
                bank_t[id(ps[1])] = ev
                X = cm(name)
                vch(V.tensor_scalar_mul, dgm[:, :], eye64f[:, :], dcol_re)
                vch(V.tensor_scalar_mul, t64c[:, :], osb[:, 0:64], scol_re_ap)
                vch(V.tensor_scalar_mul, t64d[:, :], osb[:, 64:128], scol_im_ap)
                vch(V.tensor_sub, t64c[:, :], t64c[:, :], t64d[:, :])
                vch(V.tensor_sub, X[:, 0:64], dgm[:, :], t64c[:, :])
                vch(V.tensor_scalar_mul, dgm[:, :], eye64f[:, :], dcol_im)
                vch(V.tensor_scalar_mul, t64c[:, :], osb[:, 64:128], scol_re_ap)
                vch(V.tensor_scalar_mul, t64d[:, :], osb[:, 0:64], scol_im_ap)
                vch(V.tensor_add, t64c[:, :], t64c[:, :], t64d[:, :])
                vch(V.tensor_sub, X[:, 64:128], dgm[:, :], t64c[:, :])
                t = vch(V.tensor_scalar_mul, X[:, 128:192], X[:, 64:128], -1.0)
                return t

            sy.waits("V", [ge64, yb2])
            vch(V.tensor_scalar_mul, C(22), SM(0), C(0))
            vch(V.tensor_scalar_add, C(22), C(22), 1.0)
            vch(V.tensor_scalar_mul, C(23), SM(1), C(0))
            vch(V.tensor_scalar_mul, C(31), C(0), 0.0)
            mat_t["M1T"] = build_M("M1T", C(5), C(6), col2[:, 16:17], col2[:, 17:18], BRE2, BIM2, BIMN, ARE, AIM)
            mat_t["M1"] = build_M("M1", C(5), C(6), col2[:, 16:17], col2[:, 17:18], ARE, AIM, AIMN, BRE2, BIM2)
            mat_t["M2"] = build_M("M2", C(22), C(23), C(0), C(31), PRE2, PIM2, PIMN, QBR, QBI)
            mat_t["M2T"] = build_M("M2T", C(22), C(23), C(0), C(31), QBR, QBI, QBIN, PRE2, PIM2)
            kstep(2)

            def cmm(dst_name, statT, movn, w=64, dst=None, want_imN=True, bank=None):
                if bank is None:
                    bank = ps[2]
                movre, movim = mre(movn), mim(movn)
                sy.waits("T", [mat_t.get(statT), mat_t.get(movn), bank_t.get(id(bank))])
                T.matmul(bank[0:64, 0:w], mre(statT), movre, start=True, stop=False)
                T.matmul(bank[0:64, 0:w], mimN(statT), movim, start=False, stop=True)
                T.matmul(bank[0:64, w:2 * w], mre(statT), movim, start=True, stop=False)
                mm4 = sy.inc("T", T.matmul(bank[0:64, w:2 * w], mim(statT), movre, start=False, stop=True))
                sy.wait("S", mm4)
                if dst is None:
                    X = cm(dst_name)
                    ev = sch(S.copy, X[:, 0:128], bank[0:64, 0:128])
                    if want_imN:
                        ev = sch(S.mul, X[:, 128:192], X[:, 64:128], -1.0)
                else:
                    ev = sch(S.copy, dst, bank[0:64, 0:2 * w])
                bank_t[id(bank)] = ev
                mat_t[dst_name] = ev
                return ev

            cmm("P0", "M1T", "M2")
            cmm("P0T", "M2", "M1T", bank=ps[3])
            sy.waits("T", [mat_t.get("P0"), bank_t.get(id(ps[1]))])
            T.matmul(ps[1][0:64, 0:1], mre("P0"), SM(6), start=True, stop=False)
            T.matmul(ps[1][0:64, 0:1], mimN("P0"), SM(12), start=False, stop=True)
            T.matmul(ps[1][0:64, 1:2], mre("P0"), SM(12), start=True, stop=False)
            cm4 = sy.inc("T", T.matmul(ps[1][0:64, 1:2], mim("P0"), SM(6), start=False, stop=True))
            sy.wait("S", cm4)
            sch(S.copy, crow[:, 0:1], ps[1][0:64, 0:1])
            t_cr = sch(S.copy, crow[:, 256:257], ps[1][0:64, 1:2])
            bank_t[id(ps[1])] = t_cr

            def wout_step(j):
                nonlocal t_cr
                w = 1 << j
                bank = ps[1]
                sy.waits("T", [mat_t.get(f"P{j}"), t_cr, bank_t.get(id(bank))])
                T.matmul(bank[0:64, 0:w], mre(f"P{j}"), crow[:, 0:w], start=True, stop=False)
                T.matmul(bank[0:64, 0:w], mimN(f"P{j}"), crow[:, 256:256 + w], start=False, stop=True)
                T.matmul(bank[0:64, w:2 * w], mre(f"P{j}"), crow[:, 256:256 + w], start=True, stop=False)
                ym = sy.inc("T", T.matmul(bank[0:64, w:2 * w], mim(f"P{j}"), crow[:, 0:w], start=False, stop=True))
                sy.wait("S", ym)
                sch(S.copy, crow[:, w:2 * w], bank[0:64, 0:w])
                t_cr = sch(S.copy, crow[:, 256 + w:256 + 2 * w], bank[0:64, w:2 * w])
                bank_t[id(bank)] = t_cr

            for j in range(1, 16):
                cmm(f"P{j}", f"P{j-1}T", f"P{j-1}")
                kstep()
                cmm(f"P{j}T", f"P{j-1}", f"P{j-1}T", bank=ps[3])
                kstep()
                if j - 1 <= 7:
                    wout_step(j - 1)

            sy.waits("T", [t_dl])
            dm_ = sy.inc("T", T.matmul(ps[0][0:64, 4:5], onesr[0:1, 0:64], SR(0), start=True, stop=True))
            sy.wait("V", dm_)
            vch(V.tensor_copy, C(26), ps[0][0:64, 4:5])
            vch(V.tensor_mul, C(24), SM(10), C(26))
            t_bd = vch(V.tensor_mul, C(25), SM(11), C(26))
            bank = ps[3]
            sy.waits("T", [t_bd, mat_t.get("M1T"), bank_t.get(id(bank))])
            T.matmul(bank[0:64, 0:1], mre("M1T"), C(24), start=True, stop=False)
            T.matmul(bank[0:64, 0:1], mimN("M1T"), C(25), start=False, stop=True)
            T.matmul(bank[0:64, 1:2], mre("M1T"), C(25), start=True, stop=False)
            bb4 = sy.inc("T", T.matmul(bank[0:64, 1:2], mim("M1T"), C(24), start=False, stop=True))
            sy.wait("S", bb4)
            sch(S.copy, col2[:, 27:29], bank[0:64, 0:2])
            t_neu = sch(S.copy, col2[:, 29:31], col2[:, 27:29])
            for it_ in range(16):
                sy.waits("T", [mat_t.get("P12T"), t_neu, bank_t.get(id(bank))])
                T.matmul(bank[0:64, 2:3], mre("P12T"), col2[:, 29:30], start=True, stop=False)
                T.matmul(bank[0:64, 2:3], mimN("P12T"), col2[:, 30:31], start=False, stop=True)
                T.matmul(bank[0:64, 3:4], mre("P12T"), col2[:, 30:31], start=True, stop=False)
                nm4 = sy.inc("T", T.matmul(bank[0:64, 3:4], mim("P12T"), col2[:, 29:30], start=False, stop=True))
                sy.wait("S", nm4)
                sch(S.activation, col2[:, 29:30], ps[3][0:64, 2:3], AF.Identity, bias=col2[:, 27:28])
                ev = sch(S.activation, col2[:, 30:31], ps[3][0:64, 3:4], AF.Identity, bias=col2[:, 28:29])
                bank_t[id(bank)] = ev
                t_neu = ev
                kstep()

            sch(S.copy, ycol[:, 0:1], col2[:, 29:30])
            t_yc = sch(S.copy, ycol[:, 256:257], col2[:, 30:31])
            for j in range(8):
                w = 1 << j
                bank = ps[0]
                sy.waits("T", [mat_t.get(f"P{j}T"), t_yc, bank_t.get(id(bank))])
                T.matmul(bank[0:64, 0:w], mre(f"P{j}T"), ycol[:, 0:w], start=True, stop=False)
                T.matmul(bank[0:64, 0:w], mimN(f"P{j}T"), ycol[:, 256:256 + w], start=False, stop=True)
                T.matmul(bank[0:64, w:2 * w], mre(f"P{j}T"), ycol[:, 256:256 + w], start=True, stop=False)
                ym = sy.inc("T", T.matmul(bank[0:64, w:2 * w], mim(f"P{j}T"), ycol[:, 0:w], start=False, stop=True))
                sy.wait("S", ym)
                sch(S.copy, ycol[:, w:2 * w], bank[0:64, 0:w])
                t_yc = sch(S.copy, ycol[:, 256 + w:256 + 2 * w], bank[0:64, w:2 * w])
                bank_t[id(bank)] = t_yc
                kstep()

            for blk, c0 in ((0, 128), (1, 0)):
                for part in range(2):
                    src = ycol[:, 256 * part + c0: 256 * part + c0 + 128]
                    sy.waits("T", [t_yc, bank_t.get(id(ps[2]))])
                    tr = sy.inc("T", T.transpose(ps[2][0:128, 0:64], src, eye64f[:, :]))
                    sy.wait("S", tr)
                    ev = sch(S.copy, ytr[:, :], ps[2][0:128, 0:64])
                    bank_t[id(ps[2])] = ev
                    sy.waits("T", [ev, bank_t.get(id(ps[3])), tk_j])
                    jm = sy.inc("T", T.matmul(ps[3][0:128, 0:64], jj[:, :], ytr[:, :], start=True, stop=True))
                    sy.wait("S", jm)
                    ev2 = sch(S.copy, win_sb[:, 128 * blk + 64 * part: 128 * blk + 64 * part + 64],
                              ps[3][0:128, 0:64])
                    bank_t[id(ps[3])] = ev2
                    kstep()
            sch(S.copy, wout_sb[0:64, 0:128], crow[:, 0:128])
            sch(S.mul, wout_sb[64:128, 0:128], crow[:, 256:384], -1.0)
            sch(S.copy, wout_sb[0:64, 128:256], crow[:, 128:256])
            t_wo = sch(S.mul, wout_sb[64:128, 128:256], crow[:, 384:512], -1.0)
            sch(S.copy, pstat[0:64, 0:64], mre("P8T"))
            sch(S.copy, pstat[0:64, 64:128], mim("P8T"))
            sch(S.mul, pstat[64:128, 0:64], mim("P8T"), -1.0)
            t_ps = sch(S.copy, pstat[64:128, 64:128], mre("P8T"))
            t_zc = vch(V.memset, carry[0][:, :], 0.0)
            while not kdone[0]:
                kstep()
        t_ssm = [t_band0, t_wo, t_ps, t_zc, sy.tok("V"), sy.tok("S")]

        r1b = sb("r1b", [128, NCH * TT], BF16)
        scr16 = sb("scr16", [128, NCH * TT], BF16)
        wtb = [sb(f"wtb{i}", [128, NCH * 512], BF16) for i in range(2)]
        outf = [sb(f"outf{i}", [128, 1024], F32) for i in range(2)]
        catb = sb("catb", [128, NCH * 36], BF16)
        bmtb = sb("bmtb", [32, D4], BF16)
        eyeb = sb("eyeb", [128, 128], BF16)
        exp4sb = sb("exp4sb", [4, 32], F32)
        mB = sb("mB", [128, TT], BF16)
        sigB = sb("sigB", [128, TT], BF16)
        statrow = sb("statrow", [1, 1024], F32)
        m_row = sb("m_row", [1, TT], F32)
        lnt2 = statrow[0:1, 0:TT]
        sigrow = statrow[0:1, TT:2 * TT]
        ginv2 = m_row[0:1, 0:TT]
        bnbuf = sb("bnbuf", [128, 48], F32)
        bnagg4 = sb("bnagg4", [128, 8], F32)
        vcol4 = sb("vcol4", [128, 4], F32)
        sigcol4 = sb("sigcol4", [128, 4], F32)
        nt = sb("nt", [128, 2048], BF16)
        caneg = sb("caneg", [1, 36], F32)
        ntf = sb("ntf", [128, 1024], F32)
        projf = ntf[0:32, 0:TT]
        geexp = ntf[0:4, TT:2 * TT]
        gates4 = geexp
        sprojb = nt[0:32, 0:TT]
        sy.waits("G", [sy.tok("V"), sy.tok("S"), sy.tok("T")])
        gdma("GP", eyeb[:, :], eyed.ap())
        gdma("GP", exp4sb[:, :], exp4d.ap())
        gdma("GP", bmtb[:, :], BmTd.ap())
        gdma("GP", catb[:, :], AP(CatA.ap().tensor, 0, [[36, 128], [128 * 36, NCH], [1, 36]]))
        tk_post = sy.tok("GP")
        tk_eye = tk_orb = tk_e4 = tk_biasb = tk_cat = tk_bmt = tk_post

        sy.waits("V", [tk_eye, t_opdcol])
        t_eyeop = vch(V.tensor_scalar_mul, eyeop[:, :], eyeb[:, :], opdcol[:, :])
        sy.waits("V", [t_band0])
        t_bf = vch(V.tensor_add, band0b[:, :], band0b[:, :], eyeop[:, :])
        sy.waits("T", [tk_cat, tk_ocb, sy.tok("S"), sy.tok("V")])
        for it in range(NCH):
            camm = T.matmul(ps[2][0:1, 0:36], onescb[:, :], catb[:, it * 36:(it + 1) * 36],
                            start=(it == 0), stop=(it == NCH - 1))
        t_ca = sy.inc("T", camm)
        sy.waits("V", [t_ca])
        t_caneg = vch(V.tensor_scalar_mul, caneg[0:1, :], ps[2][0:1, 0:36], -1.0)
        t_ssm = t_ssm + [t_bf, t_caneg]
        t_postsetup = [sy.tok("V"), sy.tok("S"), sy.tok("T")]

        bank_w = {}
        bank_r = {}
        uq_tok = {0: tk_u0, 1: tk_u0b, 2: None, 3: None}
        ln2_pend = None

        def ln2_stats(pend):
            tp, fV, fS = pend
            sy.waits("V", [fV, fS])
            for tb in range(4):
                base = tb * D4
                for kq in range(8):
                    vch(V.bn_stats, bnbuf[:, kq * 6:(kq + 1) * 6],
                        scr16[:, base + kq * 512: base + (kq + 1) * 512])
                vch(V.bn_aggr, bnagg4[:, 2 * tb:2 * tb + 2], bnbuf[:, :])
            va = AP(bnagg4[:, :].tensor, 1, [[8, 128], [2, 4]])
            return vch(V.tensor_scalar_add, vcol4[:, :], va, EPS)

        def ln2_scal(t_vc):
            sy.wait("S", t_vc)
            sch(S.activation, vcol4[:, :], vcol4[:, :], AF.Ln)
            return sch(S.activation, sigcol4[:, :], vcol4[:, :], AF.Exp, scale=-0.5)

        def ln2_norm(pend, t_sc):
            tp = pend[0]
            sy.wait("V", t_sc)
            for tb in range(4):
                base = tb * D4
                for hf in range(4):
                    op = outf[hf % 2]
                    sy.waits("V", [bank_w.get(("outf", hf % 2))])
                    t_of = vch(V.tensor_scalar, op[:, :],
                               scr16[:, base + hf * 1024: base + (hf + 1) * 1024],
                               bnagg4[:, 2 * tb:2 * tb + 1], sigcol4[:, tb:tb + 1],
                               OP.subtract, op1=OP.mult)
                    sy.wait("Y", t_of)
                    bank_w[("outf", hf % 2)] = ydma(f"YO{hf % 2}",
                                                    AP(out_d.ap().tensor,
                                                       (tp * TT + tb * 128) * D4 + hf * 1024,
                                                       [[D4, 128], [1, 1024]]), op[:, :])
            bank_r["scr16"] = sy.tok("V")

        for t in range(ntile):
            def w_dma(os):
                par = os % 2
                sy.waits("G", [bank_w.get(("wtb", par)), bank_w.get(("wexp", par)),
                               bank_w.get(("wadd", par))] + t_postsetup)
                if t == 0:
                    tk = gdma(f"GW{par}", wtb[par][:, :],
                              AP(WT.ap().tensor, os * 512,
                                 [[D4, 128], [128 * D4, NCH], [1, 512]]))
                    sy.waits("V", [tk, tk_eye])
                    for jjj in range(4):
                        it = os * 4 + jjj
                        dsl = slice(it * 512 + jjj * 128, it * 512 + (jjj + 1) * 128)
                        tk = vch(V.tensor_add, wtb[par][:, dsl], wtb[par][:, dsl], eyeb[:, :])
                    bank_w[("wadd", par)] = tk
                    bank_w[("wdma", os)] = tk
                else:
                    bank_w[("wdma", os)] = gdma(f"GW{par}", wtb[par][:, :], wscr.ap()[os, :])

            t_vc2 = ln2_stats(ln2_pend) if ln2_pend else None

            sy.waits("T", t_ssm + [bank_r.get("carrynxt0")])
            stat_dep = [None] * 16
            stats_fin = [None, None]

            def st_mm(c_):
                evA, sqA, evB, sqB = stat_dep[c_]
                first, last = (c_ == 0), (c_ == 15)
                deps = [evA, evB, sqA, sqB]
                if first:
                    deps += [bank_r.get(6), bank_r.get(7)]
                sy.waits("T", deps)
                for k2 in range(2):
                    dsl = slice((2 * c_ + k2) * TT, (2 * c_ + k2 + 1) * TT)
                    ssl = slice(((2 * c_ + k2) % 16) * TT, ((2 * c_ + k2) % 16 + 1) * TT)
                    mm1 = T.matmul(ps[6][0:1, 0:512], onescb[:, :], r1b[:, dsl],
                                   start=(first and k2 == 0), stop=(last and k2 == 1))
                    mm2 = T.matmul(ps[7][0:1, 0:512], onescb[:, :], uTb[:, ssl],
                                   start=(first and k2 == 0), stop=(last and k2 == 1))
                    if last and k2 == 1:
                        stats_fin[0] = sy.inc("T", mm1)
                        stats_fin[1] = sy.inc("T", mm2)
                    elif k2 == 1:
                        stats_fin[0] = sy.inc("T", mm2)

            for c in range(16):
                cur, nxt = c % 2, (c + 1) % 2
                biA, biB = c % 2, 2 + c % 2
                slA = slice(((2 * c) % 16) * TT, ((2 * c) % 16 + 1) * TT)
                slB = slice(((2 * c + 1) % 16) * TT, ((2 * c + 1) % 16 + 1) * TT)
                rsA = slice((2 * c) * TT, (2 * c + 1) * TT)
                rsB = slice((2 * c + 1) * TT, (2 * c + 2) * TT)
                sy.waits("T", [uq_tok[c // 4]])
                sy.waits("T", [bank_r.get("psC"), bank_r.get(4)])
                T.matmul(ps[4][0:128, 0:512], pstat[:, :], carry[cur][:, :], start=True, stop=False)
                T.matmul(ps[4][0:128, 0:512], win_sb[:, 0:128], uTb[:, slA], start=False, stop=False)
                cmt = sy.inc("T", T.matmul(ps[4][0:128, 0:512], win_sb[:, 128:256], uTb[:, slB], start=False, stop=True))
                bank_w["psC"] = cmt
                sy.waits("T", [bank_r.get(biA)])
                T.matmul(ps[biA][0:128, 0:512], band0b[:, :], uTb[:, slA], start=True, stop=False)
                amt = sy.inc("T", T.matmul(ps[biA][0:128, 0:512], wout_sb[:, 0:128], carry[cur][:, :], start=False, stop=True))
                bank_w[biA] = amt
                sy.waits("T", [bank_r.get(biB)])
                T.matmul(ps[biB][0:128, 0:512], band1b[:, :], uTb[:, slA], start=True, stop=False)
                T.matmul(ps[biB][0:128, 0:512], band0b[:, :], uTb[:, slB], start=False, stop=False)
                bmt = sy.inc("T", T.matmul(ps[biB][0:128, 0:512], wout_sb[:, 128:256], carry[cur][:, :], start=False, stop=True))
                bank_w[biB] = bmt
                if c > 0:
                    st_mm(c - 1)
                sy.waits("S", [cmt, bank_r.get("carrynxt%d" % nxt)])
                cv = sch(S.copy, carry[nxt][:, :], ps[4][0:128, 0:512])
                bank_r["psC"] = cv
                bank_r[4] = cv
                bank_r["carrynxt%d" % nxt] = cv
                sy.waits("S", [amt, bmt])
                evA = sch(S.copy, r1b[:, rsA], ps[biA][0:128, 0:512])
                sqA = sch(S.square, uTb[:, slA], r1b[:, rsA])
                bank_r[biA] = evA
                evB = sch(S.copy, r1b[:, rsB], ps[biB][0:128, 0:512])
                sqB = sch(S.square, uTb[:, slB], r1b[:, rsB])
                bank_r[biB] = evB
                stat_dep[c] = (evA, sqA, evB, sqB)
                if c == 4:
                    uq_tok[2] = u_dma(t, 2, [sy.tok("T"), sy.tok("S"), sy.tok("V")])
                if c == 8:
                    uq_tok[3] = u_dma(t, 3, [sy.tok("T"), sy.tok("S"), sy.tok("V")])
                if c == 11:
                    w_dma(0)
                    w_dma(1)
                if c == 12 and ln2_pend is not None:
                    t_sc2 = ln2_scal(t_vc2)
            st_mm(15)
            t_s1 = stats_fin[0]
            t_s2 = stats_fin[1]
            t_conv = sy.tok("V")
            t_convT = sy.tok("T")
            t_convS = sy.tok("S")
            sy.waits("V", [t_convS, t_convT])
            zc = vch(V.memset, carry[0][:, :], 0.0)
            bank_r["carrynxt0"] = zc

            sy.waits("V", [t_s1, t_s2])
            t_mr = vch(V.tensor_scalar_mul, m_row[0:1, :], ps[6][0:1, 0:512], 1.0 / D4)
            vch(V.tensor_mul, lnt2, m_row[0:1, :], m_row[0:1, :])
            e_st = vch(V.scalar_tensor_tensor, lnt2, ps[7][0:1, 0:512], 1.0 / D4, lnt2,
                       OP.mult, OP.subtract)
            bank_r[6] = e_st
            bank_r[7] = e_st
            t_var = vch(V.tensor_scalar_add, lnt2, lnt2, EPS)
            sy.wait("S", t_var)
            sch(S.activation, lnt2, lnt2, AF.Ln)
            t_sig = sch(S.activation, sigrow, lnt2, AF.Exp, scale=-0.5)
            sy.waits("T", [bank_r.get(2), tk_cat])
            for it in range(NCH):
                T.matmul(ps[2][0:36, 0:512], catb[:, it * 36:(it + 1) * 36],
                         r1b[:, it * TT:(it + 1) * TT], start=(it == 0), stop=False)
            sy.waits("T", [t_mr])
            t_pj = sy.inc("T", T.matmul(ps[2][0:36, 0:512], caneg[0:1, :], m_row[0:1, :],
                                        start=False, stop=True))
            sy.waits("T", [bank_r.get(0), bank_r.get(1), sy.tok("V"), t_sig])
            m1 = sy.inc("T", T.matmul(ps[0][0:128, 0:512], onesr[0:1, :], m_row[0:1, :], start=True, stop=True))
            m2 = sy.inc("T", T.matmul(ps[1][0:128, 0:512], onesr[0:1, :], sigrow, start=True, stop=True))
            bank_w[0], bank_w[1] = m1, m2
            sy.waits("S", [m1, m2])
            sch(S.copy, mB[:, :], ps[0][0:128, 0:512])
            t_sigB = sch(S.copy, sigB[:, :], ps[1][0:128, 0:512])
            bank_r[0] = t_sigB
            bank_r[1] = t_sigB
            sy.waits("V", [t_sigB, t_s1, t_s2, t_convS])
            mB4 = AP(mB[:, :].tensor, 0, [[512, 128], [0, 4], [1, 512]])
            sigB4 = AP(sigB[:, :].tensor, 0, [[512, 128], [0, 4], [1, 512]])
            sy.waits("T", [bank_r[0], tk_cat, bank_w.get(("wdma", 0))])
            e_pj = t_gv = None
            for cq in range(8):
                cs = slice(cq * 2048, (cq + 1) * 2048)
                vch(V.tensor_sub, nt[:, :], r1b[:, cs], mB4)
                tnq = vch(V.tensor_mul, r1b[:, cs], nt[:, :], sigB4)
                if cq == 3:
                    sy.wait("V", t_pj)
                    e_pj = vch(V.tensor_copy, projf, ps[2][0:32, 0:512])
                    t_gv = vch(V.tensor_mul, geexp, ps[2][32:36, 0:512], sigB[0:4, :])
                    bank_r[2] = t_gv
                    sy.waits("S", [t_gv])
                    t_ge = sch(S.activation, geexp, geexp, AF.Exp)
                sy.waits("T", [tnq])
                for it in range(4 * cq, 4 * cq + 4):
                    T.matmul(ps[0][0:128, 0:512], r1b[:, it * TT: it * TT + 128],
                             wtb[0][:, it * 512:(it + 1) * 512], start=(it == 0), stop=False)
                if cq == 3:
                    sy.waits("T", [t_ge, tk_oc, bank_r.get(7)])
                    t_gs = sy.inc("T", T.matmul(ps[7][0:1, 0:512], onesc[0:4, 0:1], geexp, start=True, stop=True))
                    bank_w[7] = t_gs
                    sy.wait("S", t_gs)
                    sch(S.activation, ginv2, ps[7][0:1, 0:512], AF.Ln)
                    t_gi = sch(S.activation, ginv2, ginv2, AF.Exp, scale=-1.0)
                    bank_r[7] = t_gi
                    sy.waits("T", [bank_r[7], t_gi])
                    t_b4 = sy.inc("T", T.matmul(ps[7][0:4, 0:512], onesr[0:1, 0:4], ginv2, start=True, stop=True))
                    bank_w[7] = t_b4
            t_x1 = sy.tok("V")
            sy.waits("V", [t_b4, t_ge])
            t_g4 = vch(V.tensor_mul, gates4, geexp, ps[7][0:4, 0:512])
            t_g4s = vch(V.tensor_mul, gates4, gates4, sigB[0:4, :])
            bank_r[7] = t_g4s
            sy.waits("T", [bank_r[7], t_g4s, tk_e4])
            t_ex = sy.inc("T", T.matmul(ps[7][0:32, 0:512], exp4sb[:, :], gates4, start=True, stop=True))
            bank_w[7] = t_ex
            sy.waits("V", [t_ex, e_pj])
            t_sp = vch(V.tensor_mul, sprojb, projf, ps[7][0:32, 0:512])
            bank_r[7] = t_sp
            if ln2_pend is not None:
                ln2_norm(ln2_pend, t_sc2)
                ln2_pend = None

            for os in range(8):
                par = os % 2
                tk_w = bank_w.pop(("wdma", os), None)
                for tb in range(4):
                    bi = (os * 4 + tb) % 8
                    sy.waits("T", [bank_r.get(bi), tk_w, t_x1, tk_setup, tk_bmt])
                    if not (os == 0 and tb == 0):
                        for it in range(NCH):
                            T.matmul(ps[bi][0:128, 0:512],
                                     r1b[:, it * TT + tb * 128: it * TT + (tb + 1) * 128],
                                     wtb[par][:, it * 512:(it + 1) * 512],
                                     start=(it == 0), stop=False)
                    T.matmul(ps[bi][0:128, 0:512], onesrb[0:1, :],
                             biasrb2[0:1, os * 512:(os + 1) * 512], start=False, stop=False)
                    sy.waits("T", [t_sp])
                    lm = T.matmul(ps[bi][0:128, 0:512],
                                  sprojb[:, tb * 128:(tb + 1) * 128],
                                  bmtb[:, os * 512:(os + 1) * 512], start=False, stop=True)
                    bank_w[bi] = sy.inc("T", lm)
                    dst = scr16[:, tb * D4 + os * 512: tb * D4 + (os + 1) * 512]
                    if (os + tb) % 2 == 0:
                        sy.waits("V", [bank_w[bi], bank_r.get("scr16")])
                        ev = vch(V.tensor_copy, dst, ps[bi][0:128, 0:512])
                    else:
                        sy.waits("S", [bank_w[bi], bank_r.get("scr16")])
                        ev = sch(S.copy, dst, ps[bi][0:128, 0:512])
                    bank_r[bi] = ev
                bank_w[("wtb", par)] = sy.tok("T")
                if t == 0:
                    sy.waits("Y", [tk_w])
                    bank_w[("wexp", par)] = ydma(f"YW{par}", wscr.ap()[os, :], wtb[par][:, :])
                if os + 2 < 8:
                    w_dma(os + 2)
                if os == 4 and t + 1 < ntile:
                    uq_tok[0] = u_dma(t + 1, 0, [t_conv, t_convT, t_convS])
                if os == 5 and t + 1 < ntile:
                    uq_tok[1] = u_dma(t + 1, 1, [])
            ln2_pend = (t, sy.tok("V"), sy.tok("S"))

        t_vc2 = ln2_stats(ln2_pend)
        t_sc2 = ln2_scal(t_vc2)
        ln2_norm(ln2_pend, t_sc2)

        for k, semh in sy.sem.items():
            if sy.cnt[k]:
                eng = sy.eng["Y"] if k[0] in "YG" else sy.eng[k]
                eng.wait_ge(semh, sy.cnt[k])
    return nc


def _host_prep(inputs, ntile):
    ntok = ntile * TT
    f32 = np.float32
    u = np.ascontiguousarray(inputs["L0_attn_u"], dtype=f32).reshape(-1, D4)
    WTh = np.ascontiguousarray(np.asarray(inputs["L0_ffn_W"], dtype=f32).T)
    CatAh = np.ascontiguousarray(np.concatenate([
        np.asarray(inputs["L0_ffn_A"], dtype=f32).transpose(2, 0, 1).reshape(D4, 32),
        np.asarray(inputs["L0_ffn_Wr"], dtype=f32).T], axis=1))
    BmTh = np.ascontiguousarray(
        np.asarray(inputs["L0_ffn_Bm"], dtype=f32).transpose(0, 2, 1).reshape(32, D4))
    biash = np.ascontiguousarray(np.asarray(inputs["L0_ffn_b"], dtype=f32))
    lam = np.asarray(inputs["L0_attn_Lambda_ri"], dtype=f32)
    pv = np.asarray(inputs["L0_attn_p_ri"], dtype=f32)
    qv = np.asarray(inputs["L0_attn_q_ri"], dtype=f32)
    ctv = np.asarray(inputs["L0_attn_Ct_ri"], dtype=f32).reshape(64, 2)
    vc = np.asarray(inputs["L0_attn_Vc_ri"], dtype=f32)
    smalls = {
        "lam_re": lam[:, 0], "lam_im": lam[:, 1],
        "p_re": pv[:, 0], "p_im": pv[:, 1],
        "q_re": qv[:, 0], "q_im": qv[:, 1],
        "ct_re": ctv[:, 0], "ct_im": ctv[:, 1],
        "Bpv": np.asarray(inputs["L0_attn_Bp"], dtype=f32).reshape(64),
    }
    smalls = {k: np.ascontiguousarray(np.concatenate([v, v])) for k, v in smalls.items()}
    VcT_re = np.ascontiguousarray(vc[..., 0].T)
    VcT_im = np.ascontiguousarray(vc[..., 1].T)
    Dh = np.ascontiguousarray(np.asarray(inputs["L0_attn_D"], dtype=f32).reshape(1))
    lsh = np.ascontiguousarray(np.asarray(inputs["L0_attn_log_step"], dtype=f32).reshape(1))

    k_ = np.arange(D4)
    Om = np.exp(-2j * np.pi * (k_ / D4))
    cval = (1.0 - Om) / (1.0 + Om)
    cc = 2.0 / (1.0 + Om)
    cvr = cval.real.astype(f32).reshape(2, 2048)
    cvi = cval.imag.astype(f32).reshape(2, 2048)
    cvb_re = np.ascontiguousarray(np.repeat(cvr, 64, axis=0))
    cvb_im = np.ascontiguousarray(np.repeat(cvi, 64, axis=0))
    cc_re = np.ascontiguousarray(cc.real.astype(f32).reshape(128, 32))
    cc_im = np.ascontiguousarray(cc.imag.astype(f32).reshape(128, 32))
    F = np.exp(2j * np.pi * np.outer(np.arange(64), np.arange(64)) / 64.0)
    Tw = np.exp(2j * np.pi * np.outer(np.arange(64), np.arange(64)) / D4)
    ftab = np.ascontiguousarray(np.concatenate([
        F.real, F.imag, -F.imag, F.real / D4, -F.imag / D4, Tw.real, Tw.imag,
    ], axis=1).astype(f32))
    J128 = np.ascontiguousarray(np.eye(128, dtype=f32)[::-1])
    eye128 = np.eye(128, dtype=f32)
    ones_row = np.ones((1, 128), dtype=f32)
    ones_col = np.ones((128, 1), dtype=f32)
    EXP4 = np.zeros((4, 32), dtype=f32)
    for e in range(4):
        EXP4[e, e * 8:(e + 1) * 8] = 2.0

    common = dict(WT=WTh, CatA=CatAh, BmTd=BmTh, biasd=biash,
                  VcT_re=VcT_re, VcT_im=VcT_im, Dv=Dh, lsd=lsh,
                  cvb_re=cvb_re, cvb_im=cvb_im, cc_re=cc_re, cc_im=cc_im,
                  ftab=ftab, J128=J128, eye128=eye128,
                  ones_row=ones_row, ones_col=ones_col, EXP4=EXP4, **smalls)
    in_maps = []
    for c in range(NCORES):
        m = dict(common)
        m["uT"] = np.ascontiguousarray(u[c * ntok:(c + 1) * ntok].T)
        in_maps.append(m)
    return in_maps


def kernel(**inputs):
    ntile = 2048 // TT
    in_maps = _host_prep(inputs, ntile)
    nc = build_nc(ntile)
    res = run_bass_kernel_spmd(nc, in_maps, core_ids=list(range(NCORES)))
    out = np.concatenate([res.results[i]["o"] for i in range(NCORES)], axis=0)
    return out.reshape(2, 8192, D4)


# version 29
# speedup vs baseline: 1.0528x; 1.0072x over previous
import contextlib
import numpy as np
import concourse.bass as bass
import concourse.mybir as mybir
from concourse.bass import AP
from concourse.bass_utils import run_bass_kernel_spmd
from concourse.alu_op_type import AluOpType as OP

F32 = mybir.dt.float32
BF16 = mybir.dt.bfloat16
AF = mybir.ActivationFunctionType

D4 = 4096
NCH = 32
TT = 512
NCORES = 8
EPS = 1e-5
LOG2 = 0.6931471805599453


class Sync:

    def __init__(self, nc, st):
        self.nc = nc
        self.st = st
        self.eng = {"T": nc.tensor, "V": nc.vector, "S": nc.scalar,
                    "G": nc.gpsimd, "Y": nc.sync}
        self.sem = {k: st.enter_context(nc.semaphore(f"sem{k}")) for k in self.eng}
        self.cnt = {k: 0 for k in self.eng}
        self.last = {}

    def new_sem(self, key):
        self.sem[key] = self.st.enter_context(self.nc.semaphore(f"sem{key}"))
        self.cnt[key] = 0

    def inc(self, k, instr, n=1):
        instr.then_inc(self.sem[k], n)
        self.cnt[k] += n
        return (k, self.cnt[k])

    def tok(self, k):
        return (k, self.cnt[k])

    def wait(self, consumer, tok):
        if tok is None:
            return
        k, v = tok
        if v <= 0:
            return
        key = (consumer, k)
        if self.last.get(key, 0) >= v:
            return
        self.eng[consumer].wait_ge(self.sem[k], v)
        self.last[key] = v

    def waits(self, consumer, toks):
        best = {}
        for t in toks:
            if t is None:
                continue
            k, v = t
            if v > best.get(k, 0):
                best[k] = v
        for k, v in best.items():
            self.wait(consumer, (k, v))


def build_nc(ntile):
    ntok = ntile * TT
    nc = bass.Bass("TRN2", target_bir_lowering=False)

    def dti(name, shp, dtype=F32, kind="ExternalInput"):
        return nc.dram_tensor(name, shp, dtype, kind=kind)

    uT = dti("uT", [D4, ntok])
    WT = dti("WT", [D4, D4])
    CatA = dti("CatA", [D4, 36])
    BmTd = dti("BmTd", [33, D4])
    biasd = dti("biasd", [D4])
    smd = {n: dti(n, [128]) for n in ["lam_re", "lam_im", "p_re", "p_im", "q_re",
                                      "q_im", "ct_re", "ct_im", "Bpv"]}
    vctre_d = dti("VcT_re", [64, 64]); vctim_d = dti("VcT_im", [64, 64])
    Dv = dti("Dv", [1]); lsd = dti("lsd", [1])
    cvbre_d = dti("cvb_re", [128, 2048]); cvbim_d = dti("cvb_im", [128, 2048])
    ccre_d = dti("cc_re", [128, 32]); ccim_d = dti("cc_im", [128, 32])
    ftd = dti("ftab", [64, 448])
    jd = dti("J128", [128, 128]); eyed = dti("eye128", [128, 128])
    onesrd = dti("ones_row", [1, 128]); onescd = dti("ones_col", [128, 1])
    exp4d = dti("EXP4", [4, 32])
    out_d = dti("o", [ntok, D4], F32, kind="ExternalOutput")
    scr = dti("scr", [20, D4], F32, kind="Internal")
    wscr = dti("wscr", [8, 128 * NCH * 512], BF16, kind="Internal")
    kpad = dti("kpad", [4224], F32, kind="Internal")

    with contextlib.ExitStack() as st:
        sy = Sync(nc, st)
        T, V, S, G, Y = (sy.eng[k] for k in "TVSGY")

        def sb(name, shp, dtype):
            return st.enter_context(nc.sbuf_tensor(name, shp, dtype))

        ps = [st.enter_context(nc.psum_tensor(f"ps{i}", [128, TT], F32)) for i in range(8)]

        def vch(fn, *a, **kw):
            t = sy.inc("V", fn(*a, **kw))
            sy.wait("V", t)
            return t

        def sch(fn, *a, **kw):
            t = sy.inc("S", fn(*a, **kw))
            sy.wait("S", t)
            return t

        for k in ["GS", "GP", "GG", "GU0", "GU1", "GW0", "GW1", "GH0", "GH1",
                  "YB", "YS0", "YS1", "YO0", "YO1", "YW0", "YW1"]:
            sy.new_sem(k)

        def gdma(key, out_ap, in_ap):
            return sy.inc(key, G.dma_start(out=out_ap, in_=in_ap), 16)

        def ydma(key, out_ap, in_ap):
            return sy.inc(key, Y.dma_start(out=out_ap, in_=in_ap), 16)

        uTb = sb("uTb", [128, 16 * TT], BF16)
        onesr = sb("onesr", [1, 128], F32)
        onesc = sb("onesc", [128, 1], F32)
        onescb = sb("onescb", [128, 1], BF16)
        smallcol = sb("smallcol", [128, 16], F32)
        dls = sb("dls", [1, 4], F32)
        sc1 = sb("sc1", [1, 4], F32)
        opdcol = sb("opdcol", [128, 1], F32)
        ztile = sb("ztile", [1, 128], F32)
        Vstack = sb("Vstack", [64, 8], F32)
        tc = sb("tc", [64, 2], F32)
        vct_re = sb("vct_re", [64, 64], F32); vct_im = sb("vct_im", [64, 64], F32)
        ccre = sb("ccre", [128, 32], F32); ccim = sb("ccim", [128, 32], F32)
        ft = sb("ft", [64, 448], F32)
        onesrb = sb("onesrb", [1, 128], BF16)
        for i, n in enumerate(["lam_re", "lam_im", "p_re", "p_im", "q_re",
                               "q_im", "ct_re", "ct_im", "Bpv"]):
            gdma("GS", smallcol[:, i:i + 1], smd[n].ap())
        gdma("GS", dls[0:1, 0:1], Dv.ap())
        gdma("GS", dls[0:1, 1:2], lsd.ap())
        gdma("GS", vct_re[:, :], vctre_d.ap())
        gdma("GS", vct_im[:, :], vctim_d.ap())
        gdma("GS", ccre[:, :], ccre_d.ap())
        gdma("GS", ccim[:, :], ccim_d.ap())
        gdma("GS", ft[:, :], ftd.ap())
        gdma("GS", onesr[0:1, :], onesrd.ap())
        gdma("GS", onesc[:, :], onescd.ap())
        gdma("GS", onescb[:, :], onescd.ap())
        gdma("GS", onesrb[0:1, :], onesrd.ap())
        t_bB = sy.tok("GS")

        tk_setup = sy.tok("GS")
        tk_small = [tk_setup] * 9
        tk_D = tk_ls = tk_vct = tk_ft = tk_or = tk_oc = tk_ocb = tk_setup
        tk_cc = [tk_setup]

        def u_dma(t, q, war_toks):
            sy.waits("G", war_toks)
            return gdma(f"GU{q % 2}", uTb[:, 8 * (q % 2) * TT: (8 * (q % 2) + 8) * TT],
                        AP(uT.ap().tensor, 8 * q * 128 * ntok + t * TT,
                           [[ntok, 128], [128 * ntok, 8], [1, TT]]))

        sy.waits("S", [tk_D, tk_ls])
        sch(S.activation, sc1[0:1, 0:1], dls[0:1, 1:2], AF.Exp, scale=-1.0)
        t_twod = sch(S.mul, sc1[0:1, 1:2], sc1[0:1, 0:1], 2.0)
        t_opd = sch(S.add, sc1[0:1, 2:3], dls[0:1, 0:1], 1.0)


        sy.waits("T", [tk_or, t_twod, t_opd])
        g1t = sy.inc("T", T.matmul(ps[0][0:128, 0:1], onesr[0:1, 0:128], sc1[0:1, 1:2], start=True, stop=True))
        g2t = sy.inc("T", T.matmul(ps[0][0:128, 1:2], onesr[0:1, 0:128], sc1[0:1, 2:3], start=True, stop=True))
        sy.waits("V", [g1t, g2t])
        t_twocol = vch(V.tensor_copy, smallcol[:, 9:10], ps[0][0:128, 0:1])
        t_opdcol = vch(V.tensor_copy, opdcol[:, :], ps[0][0:128, 1:2])

        sy.waits("T", [tk_vct, tk_small[8], t_twocol, t_opdcol])
        bt1 = sy.inc("T", T.matmul(ps[0][0:64, 0:1], vct_re[:, :], smallcol[0:64, 8:9], start=True, stop=True))
        bt2 = sy.inc("T", T.matmul(ps[0][0:64, 1:2], vct_im[:, :], smallcol[0:64, 8:9], start=True, stop=True))
        sy.waits("V", [bt1, bt2])
        vch(V.tensor_copy, smallcol[0:64, 10:12], ps[0][0:64, 0:2])

        sy.waits("V", [tk_small[7], tk_small[5]])
        vch(V.tensor_scalar_mul, smallcol[0:64, 12:13], smallcol[0:64, 7:8], -1.0)
        vch(V.tensor_scalar_mul, smallcol[0:64, 13:14], smallcol[0:64, 5:6], -1.0)
        sy.waits("V", tk_small[:7])

        BRE, BIM = smallcol[0:64, 10:11], smallcol[0:64, 11:12]
        PRE, PIM = smallcol[0:64, 2:3], smallcol[0:64, 3:4]
        CTR, CTI = smallcol[0:64, 6:7], smallcol[0:64, 12:13]
        QR, QI = smallcol[0:64, 4:5], smallcol[0:64, 13:14]

        def cmul_into(dre, dim_, ar, ai, br, bi):
            vch(V.tensor_mul, tc[:, 0:1], ar, br)
            vch(V.tensor_mul, tc[:, 1:2], ai, bi)
            vch(V.tensor_sub, dre, tc[:, 0:1], tc[:, 1:2])
            vch(V.tensor_mul, tc[:, 0:1], ar, bi)
            vch(V.tensor_mul, tc[:, 1:2], ai, br)
            vch(V.tensor_add, dim_, tc[:, 0:1], tc[:, 1:2])

        cmul_into(Vstack[:, 0:1], Vstack[:, 1:2], CTR, CTI, BRE, BIM)
        cmul_into(Vstack[:, 2:3], Vstack[:, 3:4], CTR, CTI, PRE, PIM)
        cmul_into(Vstack[:, 4:5], Vstack[:, 5:6], QR, QI, BRE, BIM)
        cmul_into(Vstack[:, 6:7], Vstack[:, 7:8], QR, QI, PRE, PIM)
        t_vstack = sy.tok("V")

        win_sb = sb("win_sb", [128, 256], BF16)
        wout_sb = sb("wout_sb", [128, 256], BF16)
        pstat = sb("pstat", [128, 128], BF16)
        band0b = sb("band0b", [128, 128], BF16)
        band1b = sb("band1b", [128, 128], BF16)
        eyeop = sb("eyeop", [128, 128], BF16)
        carry = [sb(f"carry{i}", [128, TT], BF16) for i in range(2)]
        jj = sb("jj", [128, 128], F32)
        hankb = [sb(f"hank{i}", [128, 128], F32) for i in range(2)]
        sy.waits("G", [sy.tok("V"), sy.tok("S")])
        tk_j = gdma("GH0", jj[:, :], jd.ap())

        tk_u0 = None
        tk_u0b = None
        t_band0 = None
        with contextlib.ExitStack() as st2:
            def sb2(name, shp, dtype=F32):
                return st2.enter_context(nc.sbuf_tensor(name, shp, dtype))

            def kpath():
                nonlocal tk_u0, tk_u0b, t_band0
                g1 = sb2("g1", [128, 2048], F32)
                g2 = sb2("g2", [128, 2048], F32)
                g3 = sb2("g3", [128, 2048], F32)
                g4 = sb2("g4", [128, 2048], F32)
                lncol = sb2("lncol", [128, 1], F32)
                vs128 = sb2("vs128", [128, 16], F32)
                stage8 = [sb2(f"stage8{i}", [16, 512], F32) for i in range(2)]
                stage_tok = [None, None]
                t_vz = vch(V.memset, vs128[:, :], 0.0)
                sy.waits("Y", [t_vstack, t_vz])
                yv1 = ydma("YB", AP(scr.ap().tensor, 0, [[1, 512]]), Vstack[:, :])
                sy.waits("Y", [yv1])
                ydma("YB", vs128[0:64, 0:8], AP(scr.ap().tensor, 0, [[8, 64], [1, 8]]))
                yv3 = ydma("YB", vs128[64:128, 8:16], AP(scr.ap().tensor, 0, [[8, 64], [1, 8]]))
                sy.waits("G", [sy.tok("T")])
                tg1 = gdma("GG", g1[:, :], cvbre_d.ap())
                tg2 = gdma("GG", g2[:, :], cvbim_d.ap())
                tk_u0 = u_dma(0, 0, [])
                tk_u0b = u_dma(0, 1, [])
                yield
                sy.waits("V", [tg1, tg2, t_twocol, tk_small[0], tk_small[1]])
                vch(V.tensor_scalar_mul, g1[:, :], g1[:, :], smallcol[:, 9:10])
                vch(V.tensor_scalar_sub, g1[:, :], g1[:, :], smallcol[:, 0:1])
                vch(V.tensor_scalar_mul, g2[:, :], g2[:, :], smallcol[:, 9:10])
                v4 = vch(V.tensor_scalar_sub, g2[:, :], g2[:, :], smallcol[:, 1:2])
                yield
                t_lnc = vch(V.memset, lncol[:, :], -55.451774444795625)
                sy.waits("S", [v4, t_lnc])
                sq1 = sch(S.activation, g3[:, :], g1[:, :], AF.Square, scale=2.0 ** -40)
                sq2 = sch(S.activation, g4[:, :], g2[:, :], AF.Square, scale=2.0 ** -40)
                yield
                sy.waits("V", [sq1, sq2])
                v5 = vch(V.tensor_add, g3[:, :], g3[:, :], g4[:, :])
                sy.wait("S", v5)
                sch(S.activation, g3[:, :], g3[:, :], AF.Ln)
                s4t = sch(S.activation, g3[:, :], g3[:, :], AF.Exp, scale=-1.0, bias=lncol[:, :])
                s5t = sch(S.mul, g2[:, :], g2[:, :], -1.0)
                yield
                sy.waits("V", [s4t, s5t])
                v6 = vch(V.tensor_mul, g1[:, :], g1[:, :], g3[:, :])
                v7 = vch(V.tensor_mul, g2[:, :], g2[:, :], g3[:, :])
                yield
                sy.waits("T", [v6, v7, yv3, t_bB])
                gbank_t = {}
                for c in range(4):
                    csl = slice(c * 512, (c + 1) * 512)
                    bank = ps[6 + c % 2]
                    for half, iv in ((0, g1), (1, g2)):
                        sg = half
                        sy.waits("T", [gbank_t.get(id(bank))])
                        m1 = sy.inc("T", T.matmul(bank[0:16, 0:512], vs128[:, :], iv[:, csl],
                                                  start=True, stop=True))
                        sy.waits("V", [m1, stage_tok[sg]])
                        sy.waits("Y", [yv3])
                        e1 = vch(V.tensor_copy, stage8[sg][:, :], bank[0:16, 0:512])
                        gbank_t[id(bank)] = e1
                        sy.wait("Y", e1)
                        base = (8 * D4 if half == 1 else 0) + c * 512
                        dtok = ydma(f"YS{sg}", AP(scr.ap().tensor, base, [[2048, 2], [D4, 8], [1, 512]]),
                                    stage8[sg][:, :])
                        stage_tok[sg] = dtok
                    yield

                kk = sb2("kk", [128, 512], F32)
                sy.waits("Y", stage_tok)
                b3 = ydma("YB", kk[:, :], AP(scr.ap().tensor, 0, [[32, 128], [D4, 16], [1, 32]]))

                ta = sb2("ta", [128, 8 * 32], F32)
                tb_ = sb2("tb_", [128, 6 * 32], F32)
                kb = lambda r: kk[:, r * 32:(r + 1) * 32]
                ab = lambda r: ta[:, r * 32:(r + 1) * 32]
                tbb = lambda r: tb_[:, r * 32:(r + 1) * 32]
                sy.wait("V", b3)
                yield
                vch(V.tensor_sub, ab(0), kb(0), kb(9));   vch(V.tensor_add, ab(1), kb(1), kb(8))
                vch(V.tensor_sub, ab(2), kb(2), kb(11));  vch(V.tensor_add, ab(3), kb(3), kb(10))
                vch(V.tensor_sub, ab(4), kb(4), kb(13));  vch(V.tensor_add, ab(5), kb(5), kb(12))
                vch(V.tensor_sub, ab(6), kb(6), kb(15));  t_ab7 = vch(V.tensor_add, ab(7), kb(7), kb(14))
                t_ab6 = vch(V.tensor_scalar_add, ab(6), ab(6), 1.0)
                yield
                sy.waits("S", [t_ab6, t_ab7])
                w1 = sch(S.square, tbb(0), ab(6))
                w2 = sch(S.square, tbb(1), ab(7))
                sy.waits("V", [w1, w2])
                w3 = vch(V.tensor_add, tbb(0), tbb(0), tbb(1))
                sy.wait("S", w3)
                sch(S.activation, tbb(0), tbb(0), AF.Ln)
                w4 = sch(S.activation, tbb(0), tbb(0), AF.Exp, scale=-1.0)
                w5 = sch(S.mul, ab(7), ab(7), -1.0)
                yield
                sy.waits("V", [w4, w5])
                vch(V.tensor_mul, ab(6), ab(6), tbb(0))
                vch(V.tensor_mul, ab(7), ab(7), tbb(0))
                vch(V.tensor_mul, tbb(1), ab(2), ab(4))
                vch(V.tensor_mul, tbb(2), ab(3), ab(5))
                vch(V.tensor_sub, tbb(1), tbb(1), tbb(2))
                vch(V.tensor_mul, tbb(2), ab(2), ab(5))
                vch(V.tensor_mul, tbb(3), ab(3), ab(4))
                vch(V.tensor_add, tbb(2), tbb(2), tbb(3))
                yield
                vch(V.tensor_mul, tbb(3), tbb(1), ab(6))
                vch(V.tensor_mul, tbb(4), tbb(2), ab(7))
                vch(V.tensor_sub, tbb(3), tbb(3), tbb(4))
                vch(V.tensor_mul, tbb(4), tbb(1), ab(7))
                vch(V.tensor_mul, tbb(5), tbb(2), ab(6))
                vch(V.tensor_add, tbb(4), tbb(4), tbb(5))
                vch(V.tensor_sub, ab(0), ab(0), tbb(3))
                vch(V.tensor_sub, ab(1), ab(1), tbb(4))
                yield
                sy.waits("V", tk_cc)
                vch(V.tensor_mul, tbb(0), ccre[:, :], ab(0))
                vch(V.tensor_mul, tbb(1), ccim[:, :], ab(1))
                vch(V.tensor_sub, tbb(0), tbb(0), tbb(1))
                vch(V.tensor_mul, tbb(1), ccre[:, :], ab(1))
                vch(V.tensor_mul, tbb(2), ccim[:, :], ab(0))
                t_ar = vch(V.tensor_add, tbb(1), tbb(1), tbb(2))
                yield

                sy.wait("Y", t_ar)
                b4 = ydma("YB", AP(scr.ap().tensor, 16 * D4, [[1, D4]]), tbb(0))
                b5 = ydma("YB", AP(scr.ap().tensor, 17 * D4, [[1, D4]]), tbb(1))
                Rre = sb2("Rre", [64, 64], F32)
                Rim = sb2("Rim", [64, 64], F32)
                sy.waits("Y", [b4, b5])
                b6 = ydma("YB", Rre[:, :], AP(scr.ap().tensor, 16 * D4, [[64, 64], [1, 64]]))
                b7 = ydma("YB", Rim[:, :], AP(scr.ap().tensor, 17 * D4, [[64, 64], [1, 64]]))
                yield

                FRE, FIM, FIMN = ft[:, 0:64], ft[:, 64:128], ft[:, 128:192]
                FCRE, FCIMN = ft[:, 192:256], ft[:, 256:320]
                TRE, TIM = ft[:, 320:384], ft[:, 384:448]
                sy.waits("T", [b6, b7, tk_ft, gbank_t.get(id(ps[6]))])
                sy.inc("T", T.matmul(ps[6][0:64, 0:64], Rre[:, :], FRE, start=True, stop=False))
                am1 = sy.inc("T", T.matmul(ps[6][0:64, 0:64], Rim[:, :], FIMN, start=False, stop=True))
                sy.inc("T", T.matmul(ps[6][0:64, 64:128], Rre[:, :], FIM, start=True, stop=False))
                am2 = sy.inc("T", T.matmul(ps[6][0:64, 64:128], Rim[:, :], FRE, start=False, stop=True))
                a1sb = sb2("a1sb", [64, 128], F32)
                a2sb = sb2("a2sb", [64, 128], F32)
                t64a = sb2("t64a", [64, 64], F32)
                t64b = sb2("t64b", [64, 64], F32)
                sy.waits("V", [am1, am2])
                vch(V.tensor_copy, a1sb[:, :], ps[6][0:64, 0:128])
                vch(V.tensor_mul, t64a[:, :], a1sb[:, 0:64], TRE)
                vch(V.tensor_mul, t64b[:, :], a1sb[:, 64:128], TIM)
                vch(V.tensor_sub, a2sb[:, 0:64], t64a[:, :], t64b[:, :])
                vch(V.tensor_mul, t64a[:, :], a1sb[:, 0:64], TIM)
                vch(V.tensor_mul, t64b[:, :], a1sb[:, 64:128], TRE)
                t_a2 = vch(V.tensor_add, a2sb[:, 64:128], t64a[:, :], t64b[:, :])
                yield
                sy.wait("T", t_a2)
                sy.inc("T", T.matmul(ps[7][0:64, 0:64], FCRE, a2sb[:, 0:64], start=True, stop=False))
                kmt = sy.inc("T", T.matmul(ps[7][0:64, 0:64], FCIMN, a2sb[:, 64:128], start=False, stop=True))
                Ksb = sb2("Ksb", [64, 64], F32)
                sy.wait("V", kmt)
                t_ksb = vch(V.tensor_copy, Ksb[:, :], ps[7][0:64, 0:64])
                yield

                t_z = vch(V.memset, ztile[0:1, :], 0.0)
                sy.waits("Y", [t_z, t_ksb])
                z1 = ydma("YB", AP(kpad.ap().tensor, 0, [[1, 127]]), ztile[0:1, 0:127])
                z2 = ydma("YB", AP(kpad.ap().tensor, 127, [[1, D4]]), Ksb[:, :])
                z3 = ydma("YB", AP(kpad.ap().tensor, 4223, [[1, 1]]), ztile[0:1, 0:1])
                tk_kpad = [z1, z2, z3]
                yield

                sy.waits("G", tk_kpad)
                hg = gdma("GH0", hankb[0][:, :], AP(kpad.ap().tensor, 0, [[1, 128], [1, 128]]))
                hg1 = gdma("GH0", hankb[1][:, :], AP(kpad.ap().tensor, 128, [[1, 128], [1, 128]]))
                hg = hg1
                sy.waits("T", [hg, tk_j])
                bm = sy.inc("T", T.matmul(ps[4][0:128, 0:128], jj[:, :], hankb[0][:, :], start=True, stop=True))
                sy.wait("S", bm)
                t_band0 = sch(S.copy, band0b[:, :], ps[4][0:128, 0:128])
                sy.waits("T", [hg1])
                bm1 = sy.inc("T", T.matmul(ps[5][0:128, 0:128], jj[:, :], hankb[1][:, :], start=True, stop=True))
                sy.wait("S", bm1)
                t_band0 = sch(S.copy, band1b[:, :], ps[5][0:128, 0:128])

                yield

            kg = kpath()
            kdone = [False]

            def kstep(n=1):
                for _ in range(n):
                    if kdone[0]:
                        return
                    if next(kg, "END") == "END":
                        kdone[0] = True

            kstep(2)
            col2 = sb2("col2", [64, 32])
            srow = sb2("srow", [1, 16])
            rows8 = sb2("rows8", [1, 768])
            eye64f = sb2("eye64f", [64, 64])
            osb = sb2("osb", [64, 128])
            dgm = sb2("dgm", [64, 64])
            t64c = sb2("t64c", [64, 64])
            t64d = sb2("t64d", [64, 64])
            ycol = sb2("ycol", [64, 512])
            crow = sb2("crow", [64, 512])
            ytr = sb2("ytr", [128, 64])
            mats = {}

            def cm(name):
                mats[name] = sb2(name, [64, 192])
                return mats[name]

            def mre(n): return mats[n][:, 0:64]
            def mim(n): return mats[n][:, 64:128]
            def mimN(n): return mats[n][:, 128:192]

            sy.waits("G", [sy.tok("V"), sy.tok("S")])
            ge64 = gdma("GH0", eye64f[:, :], eyed.ap()[0:64, 0:64])

            C = lambda i: col2[:, i:i + 1]
            SM = lambda i: smallcol[0:64, i:i + 1]
            SR = lambda i: srow[0:1, i:i + 1]
            t_dl = sch(S.activation, SR(0), dls[0:1, 1:2], AF.Exp)
            t_hd = sch(S.mul, SR(1), SR(0), 0.5)
            sy.waits("T", [t_hd, tk_or, sy.tok("V")])
            hm = sy.inc("T", T.matmul(ps[0][0:64, 0:1], onesr[0:1, 0:64], SR(1), start=True, stop=True))
            sy.wait("V", hm)
            vch(V.tensor_copy, C(0), ps[0][0:64, 0:1])
            vch(V.tensor_scalar_mul, C(1), SM(0), C(0))
            vch(V.tensor_scalar_mul, C(1), C(1), -1.0)
            vch(V.tensor_scalar_add, C(1), C(1), 1.0)
            vch(V.tensor_scalar_mul, C(2), SM(1), C(0))
            vch(V.tensor_scalar_mul, C(2), C(2), -1.0)
            vch(V.tensor_mul, C(3), C(1), C(1))
            vch(V.tensor_mul, C(4), C(2), C(2))
            vch(V.tensor_add, C(3), C(3), C(4))
            vch(V.reciprocal, C(4), C(3))
            vch(V.tensor_mul, C(5), C(1), C(4))
            vch(V.tensor_mul, C(6), C(2), C(4))
            vch(V.tensor_scalar_mul, C(6), C(6), -1.0)

            def cmul_cols(dr, di, ar, ai, br, bi):
                vch(V.tensor_mul, C(14), ar, br)
                vch(V.tensor_mul, C(15), ai, bi)
                vch(V.tensor_sub, dr, C(14), C(15))
                vch(V.tensor_mul, C(14), ar, bi)
                vch(V.tensor_mul, C(15), ai, br)
                vch(V.tensor_add, di, C(14), C(15))

            cmul_cols(C(7), C(8), C(5), C(6), SM(2), SM(3))
            cmul_cols(C(9), C(10), SM(4), SM(13), C(5), C(6))
            cmul_cols(C(11), C(12), C(7), C(8), SM(4), SM(13))
            t_w = sy.tok("V")
            sy.waits("T", [t_w, tk_oc])
            sm0 = sy.inc("T", T.matmul(ps[0][0:1, 0:2], onesc[0:64, 0:1], col2[:, 11:13], start=True, stop=True))
            sy.wait("V", sm0)
            vch(V.tensor_copy, srow[0:1, 2:4], ps[0][0:1, 0:2])
            vch(V.tensor_mul, SR(4), SR(1), SR(2))
            vch(V.tensor_scalar_add, SR(4), SR(4), 1.0)
            vch(V.tensor_mul, SR(5), SR(1), SR(3))
            vch(V.tensor_mul, SR(6), SR(4), SR(4))
            vch(V.tensor_mul, SR(7), SR(5), SR(5))
            vch(V.tensor_add, SR(6), SR(6), SR(7))
            vch(V.reciprocal, SR(7), SR(6))
            vch(V.tensor_mul, SR(8), SR(4), SR(7))
            vch(V.tensor_mul, SR(8), SR(8), SR(1))
            vch(V.tensor_mul, SR(9), SR(5), SR(7))
            vch(V.tensor_mul, SR(9), SR(9), SR(1))
            t_s = vch(V.tensor_scalar_mul, SR(9), SR(9), -1.0)
            sy.waits("T", [t_s])
            sb1 = sy.inc("T", T.matmul(ps[0][0:64, 2:4], onesr[0:1, 0:64], srow[0:1, 8:10], start=True, stop=True))
            sy.wait("V", sb1)
            vch(V.tensor_copy, col2[:, 16:18], ps[0][0:64, 2:4])
            vch(V.tensor_scalar_mul, C(18), C(8), -1.0)
            vch(V.tensor_scalar_mul, C(19), C(10), -1.0)
            vch(V.tensor_scalar_mul, C(20), SM(3), -1.0)
            t_cols = vch(V.tensor_scalar_mul, C(21), SM(13), -1.0)
            sy.wait("Y", t_cols)
            yb = None
            rowsrc = [C(7), C(8), C(9), C(10), SM(2), SM(3), SM(4), SM(13),
                      C(18), C(19), C(20), C(21)]
            sy.waits("Y", [sy.tok("YB")])
            for i, src in enumerate(rowsrc):
                yb = ydma("YB", AP(scr.ap().tensor, 18 * D4 + 64 * i, [[1, 64]]), src)
            sy.waits("Y", [yb])
            yb2 = ydma("YB", rows8[0:1, :], AP(scr.ap().tensor, 18 * D4, [[1, 768]]))
            R = lambda i: rows8[0:1, i * 64:(i + 1) * 64]
            ARE, AIM, BRE2, BIM2 = R(0), R(1), R(2), R(3)
            PRE2, PIM2, QBR, QBI = R(4), R(5), R(6), R(7)
            AIMN, BIMN, PIMN, QBIN = R(8), R(9), R(10), R(11)

            bank_t = {}
            mat_t = {}

            def outer_pair(bank, ur, uim, uimN, vr, vi, extra=()):
                sy.waits("T", [yb2, *extra, bank_t.get(id(bank))])
                T.matmul(bank[0:64, 0:64], ur, vr, start=True, stop=False)
                T.matmul(bank[0:64, 0:64], uimN, vi, start=False, stop=True)
                T.matmul(bank[0:64, 64:128], ur, vi, start=True, stop=False)
                mm4 = sy.inc("T", T.matmul(bank[0:64, 64:128], uim, vr, start=False, stop=True))
                return mm4

            def build_M(name, dcol_re, dcol_im, scol_re_ap, scol_im_ap,
                        ur, uim, uimN, vr, vi):
                mm4 = outer_pair(ps[1], ur, uim, uimN, vr, vi)
                sy.wait("V", mm4)
                ev = vch(V.tensor_copy, osb[:, :], ps[1][0:64, 0:128])
                bank_t[id(ps[1])] = ev
                X = cm(name)
                vch(V.tensor_scalar_mul, dgm[:, :], eye64f[:, :], dcol_re)
                vch(V.tensor_scalar_mul, t64c[:, :], osb[:, 0:64], scol_re_ap)
                vch(V.tensor_scalar_mul, t64d[:, :], osb[:, 64:128], scol_im_ap)
                vch(V.tensor_sub, t64c[:, :], t64c[:, :], t64d[:, :])
                vch(V.tensor_sub, X[:, 0:64], dgm[:, :], t64c[:, :])
                vch(V.tensor_scalar_mul, dgm[:, :], eye64f[:, :], dcol_im)
                vch(V.tensor_scalar_mul, t64c[:, :], osb[:, 64:128], scol_re_ap)
                vch(V.tensor_scalar_mul, t64d[:, :], osb[:, 0:64], scol_im_ap)
                vch(V.tensor_add, t64c[:, :], t64c[:, :], t64d[:, :])
                vch(V.tensor_sub, X[:, 64:128], dgm[:, :], t64c[:, :])
                t = vch(V.tensor_scalar_mul, X[:, 128:192], X[:, 64:128], -1.0)
                return t

            sy.waits("V", [ge64, yb2])
            vch(V.tensor_scalar_mul, C(22), SM(0), C(0))
            vch(V.tensor_scalar_add, C(22), C(22), 1.0)
            vch(V.tensor_scalar_mul, C(23), SM(1), C(0))
            vch(V.tensor_scalar_mul, C(31), C(0), 0.0)
            mat_t["M1T"] = build_M("M1T", C(5), C(6), col2[:, 16:17], col2[:, 17:18], BRE2, BIM2, BIMN, ARE, AIM)
            mat_t["M1"] = build_M("M1", C(5), C(6), col2[:, 16:17], col2[:, 17:18], ARE, AIM, AIMN, BRE2, BIM2)
            mat_t["M2"] = build_M("M2", C(22), C(23), C(0), C(31), PRE2, PIM2, PIMN, QBR, QBI)
            mat_t["M2T"] = build_M("M2T", C(22), C(23), C(0), C(31), QBR, QBI, QBIN, PRE2, PIM2)
            kstep(2)

            def cmm(dst_name, statT, movn, w=64, dst=None, want_imN=True, bank=None):
                if bank is None:
                    bank = ps[2]
                movre, movim = mre(movn), mim(movn)
                sy.waits("T", [mat_t.get(statT), mat_t.get(movn), bank_t.get(id(bank))])
                T.matmul(bank[0:64, 0:w], mre(statT), movre, start=True, stop=False)
                T.matmul(bank[0:64, 0:w], mimN(statT), movim, start=False, stop=True)
                T.matmul(bank[0:64, w:2 * w], mre(statT), movim, start=True, stop=False)
                mm4 = sy.inc("T", T.matmul(bank[0:64, w:2 * w], mim(statT), movre, start=False, stop=True))
                sy.wait("S", mm4)
                if dst is None:
                    X = cm(dst_name)
                    ev = sch(S.copy, X[:, 0:128], bank[0:64, 0:128])
                    if want_imN:
                        ev = sch(S.mul, X[:, 128:192], X[:, 64:128], -1.0)
                else:
                    ev = sch(S.copy, dst, bank[0:64, 0:2 * w])
                bank_t[id(bank)] = ev
                mat_t[dst_name] = ev
                return ev

            cmm("P0", "M1T", "M2")
            cmm("P0T", "M2", "M1T", bank=ps[3])
            sy.waits("T", [mat_t.get("P0"), bank_t.get(id(ps[1]))])
            T.matmul(ps[1][0:64, 0:1], mre("P0"), SM(6), start=True, stop=False)
            T.matmul(ps[1][0:64, 0:1], mimN("P0"), SM(12), start=False, stop=True)
            T.matmul(ps[1][0:64, 1:2], mre("P0"), SM(12), start=True, stop=False)
            cm4 = sy.inc("T", T.matmul(ps[1][0:64, 1:2], mim("P0"), SM(6), start=False, stop=True))
            sy.wait("S", cm4)
            sch(S.copy, crow[:, 0:1], ps[1][0:64, 0:1])
            t_cr = sch(S.copy, crow[:, 256:257], ps[1][0:64, 1:2])
            bank_t[id(ps[1])] = t_cr

            def wout_step(j):
                nonlocal t_cr
                w = 1 << j
                bank = ps[1]
                sy.waits("T", [mat_t.get(f"P{j}"), t_cr, bank_t.get(id(bank))])
                T.matmul(bank[0:64, 0:w], mre(f"P{j}"), crow[:, 0:w], start=True, stop=False)
                T.matmul(bank[0:64, 0:w], mimN(f"P{j}"), crow[:, 256:256 + w], start=False, stop=True)
                T.matmul(bank[0:64, w:2 * w], mre(f"P{j}"), crow[:, 256:256 + w], start=True, stop=False)
                ym = sy.inc("T", T.matmul(bank[0:64, w:2 * w], mim(f"P{j}"), crow[:, 0:w], start=False, stop=True))
                sy.wait("S", ym)
                sch(S.copy, crow[:, w:2 * w], bank[0:64, 0:w])
                t_cr = sch(S.copy, crow[:, 256 + w:256 + 2 * w], bank[0:64, w:2 * w])
                bank_t[id(bank)] = t_cr

            for j in range(1, 16):
                cmm(f"P{j}", f"P{j-1}T", f"P{j-1}")
                kstep()
                cmm(f"P{j}T", f"P{j-1}", f"P{j-1}T", bank=ps[3])
                kstep()
                if j - 1 <= 7:
                    wout_step(j - 1)

            sy.waits("T", [t_dl])
            dm_ = sy.inc("T", T.matmul(ps[0][0:64, 4:5], onesr[0:1, 0:64], SR(0), start=True, stop=True))
            sy.wait("V", dm_)
            vch(V.tensor_copy, C(26), ps[0][0:64, 4:5])
            vch(V.tensor_mul, C(24), SM(10), C(26))
            t_bd = vch(V.tensor_mul, C(25), SM(11), C(26))
            bank = ps[3]
            sy.waits("T", [t_bd, mat_t.get("M1T"), bank_t.get(id(bank))])
            T.matmul(bank[0:64, 0:1], mre("M1T"), C(24), start=True, stop=False)
            T.matmul(bank[0:64, 0:1], mimN("M1T"), C(25), start=False, stop=True)
            T.matmul(bank[0:64, 1:2], mre("M1T"), C(25), start=True, stop=False)
            bb4 = sy.inc("T", T.matmul(bank[0:64, 1:2], mim("M1T"), C(24), start=False, stop=True))
            sy.wait("S", bb4)
            sch(S.copy, col2[:, 27:29], bank[0:64, 0:2])
            t_neu = sch(S.copy, col2[:, 29:31], col2[:, 27:29])
            for it_ in range(16):
                sy.waits("T", [mat_t.get("P12T"), t_neu, bank_t.get(id(bank))])
                T.matmul(bank[0:64, 2:3], mre("P12T"), col2[:, 29:30], start=True, stop=False)
                T.matmul(bank[0:64, 2:3], mimN("P12T"), col2[:, 30:31], start=False, stop=True)
                T.matmul(bank[0:64, 3:4], mre("P12T"), col2[:, 30:31], start=True, stop=False)
                nm4 = sy.inc("T", T.matmul(bank[0:64, 3:4], mim("P12T"), col2[:, 29:30], start=False, stop=True))
                sy.wait("S", nm4)
                sch(S.activation, col2[:, 29:30], ps[3][0:64, 2:3], AF.Identity, bias=col2[:, 27:28])
                ev = sch(S.activation, col2[:, 30:31], ps[3][0:64, 3:4], AF.Identity, bias=col2[:, 28:29])
                bank_t[id(bank)] = ev
                t_neu = ev
                kstep()

            sch(S.copy, ycol[:, 0:1], col2[:, 29:30])
            t_yc = sch(S.copy, ycol[:, 256:257], col2[:, 30:31])
            for j in range(8):
                w = 1 << j
                bank = ps[0]
                sy.waits("T", [mat_t.get(f"P{j}T"), t_yc, bank_t.get(id(bank))])
                T.matmul(bank[0:64, 0:w], mre(f"P{j}T"), ycol[:, 0:w], start=True, stop=False)
                T.matmul(bank[0:64, 0:w], mimN(f"P{j}T"), ycol[:, 256:256 + w], start=False, stop=True)
                T.matmul(bank[0:64, w:2 * w], mre(f"P{j}T"), ycol[:, 256:256 + w], start=True, stop=False)
                ym = sy.inc("T", T.matmul(bank[0:64, w:2 * w], mim(f"P{j}T"), ycol[:, 0:w], start=False, stop=True))
                sy.wait("S", ym)
                sch(S.copy, ycol[:, w:2 * w], bank[0:64, 0:w])
                t_yc = sch(S.copy, ycol[:, 256 + w:256 + 2 * w], bank[0:64, w:2 * w])
                bank_t[id(bank)] = t_yc
                kstep()

            for blk, c0 in ((0, 128), (1, 0)):
                for part in range(2):
                    src = ycol[:, 256 * part + c0: 256 * part + c0 + 128]
                    sy.waits("T", [t_yc, bank_t.get(id(ps[2]))])
                    tr = sy.inc("T", T.transpose(ps[2][0:128, 0:64], src, eye64f[:, :]))
                    sy.wait("S", tr)
                    ev = sch(S.copy, ytr[:, :], ps[2][0:128, 0:64])
                    bank_t[id(ps[2])] = ev
                    sy.waits("T", [ev, bank_t.get(id(ps[3])), tk_j])
                    jm = sy.inc("T", T.matmul(ps[3][0:128, 0:64], jj[:, :], ytr[:, :], start=True, stop=True))
                    sy.wait("S", jm)
                    ev2 = sch(S.copy, win_sb[:, 128 * blk + 64 * part: 128 * blk + 64 * part + 64],
                              ps[3][0:128, 0:64])
                    bank_t[id(ps[3])] = ev2
                    kstep()
            sch(S.copy, wout_sb[0:64, 0:128], crow[:, 0:128])
            sch(S.mul, wout_sb[64:128, 0:128], crow[:, 256:384], -1.0)
            sch(S.copy, wout_sb[0:64, 128:256], crow[:, 128:256])
            t_wo = sch(S.mul, wout_sb[64:128, 128:256], crow[:, 384:512], -1.0)
            sch(S.copy, pstat[0:64, 0:64], mre("P8T"))
            sch(S.copy, pstat[0:64, 64:128], mim("P8T"))
            sch(S.mul, pstat[64:128, 0:64], mim("P8T"), -1.0)
            t_ps = sch(S.copy, pstat[64:128, 64:128], mre("P8T"))
            t_zc = vch(V.memset, carry[0][:, :], 0.0)
            while not kdone[0]:
                kstep()
        t_ssm = [t_band0, t_wo, t_ps, t_zc, sy.tok("V"), sy.tok("S")]

        r1b = sb("r1b", [128, NCH * TT], BF16)
        scr16 = sb("scr16", [128, NCH * TT], BF16)
        wtb = [sb(f"wtb{i}", [128, NCH * 512], BF16) for i in range(2)]
        outf = [sb(f"outf{i}", [128, 1024], F32) for i in range(2)]
        catb = sb("catb", [128, NCH * 36], BF16)
        bmtb = sb("bmtb", [33, D4], BF16)
        sproj2 = sb("sproj2", [33, TT], BF16)
        eyeb = sb("eyeb", [128, 128], BF16)
        exp4sb = sb("exp4sb", [4, 32], F32)
        mB = sb("mB", [128, TT], BF16)
        sigB = sb("sigB", [128, TT], BF16)
        statrow = sb("statrow", [1, 1024], F32)
        m_row = sb("m_row", [1, TT], F32)
        lnt2 = statrow[0:1, 0:TT]
        sigrow = statrow[0:1, TT:2 * TT]
        ginv2 = m_row[0:1, 0:TT]
        bnbuf = sb("bnbuf", [128, 48], F32)
        bnagg4 = sb("bnagg4", [128, 8], F32)
        vcol4 = sb("vcol4", [128, 4], F32)
        sigcol4 = sb("sigcol4", [128, 4], F32)
        nt = sb("nt", [128, 2048], BF16)
        caneg = sb("caneg", [1, 36], F32)
        ntf = sb("ntf", [128, 1024], F32)
        projf = ntf[0:32, 0:TT]
        geexp = ntf[0:4, TT:2 * TT]
        gates4 = geexp
        sprojb = sproj2[0:32, 0:TT]
        sy.waits("G", [sy.tok("V"), sy.tok("S"), sy.tok("T")])
        gdma("GP", eyeb[:, :], eyed.ap())
        gdma("GP", exp4sb[:, :], exp4d.ap())
        gdma("GP", bmtb[:, :], BmTd.ap())
        t_onerow = vch(V.memset, sproj2[32:33, :], 1.0)
        gdma("GP", catb[:, :], AP(CatA.ap().tensor, 0, [[36, 128], [128 * 36, NCH], [1, 36]]))
        tk_post = sy.tok("GP")
        tk_eye = tk_orb = tk_e4 = tk_biasb = tk_cat = tk_bmt = tk_post

        sy.waits("V", [tk_eye, t_opdcol])
        t_eyeop = vch(V.tensor_scalar_mul, eyeop[:, :], eyeb[:, :], opdcol[:, :])
        sy.waits("V", [t_band0])
        t_bf = vch(V.tensor_add, band0b[:, :], band0b[:, :], eyeop[:, :])
        sy.waits("T", [tk_cat, tk_ocb, sy.tok("S"), sy.tok("V")])
        for it in range(NCH):
            camm = T.matmul(ps[2][0:1, 0:36], onescb[:, :], catb[:, it * 36:(it + 1) * 36],
                            start=(it == 0), stop=(it == NCH - 1))
        t_ca = sy.inc("T", camm)
        sy.waits("V", [t_ca])
        t_caneg = vch(V.tensor_scalar_mul, caneg[0:1, :], ps[2][0:1, 0:36], -1.0)
        t_ssm = t_ssm + [t_bf, t_caneg]
        t_postsetup = [sy.tok("V"), sy.tok("S"), sy.tok("T")]

        bank_w = {}
        bank_r = {}
        uq_tok = {0: tk_u0, 1: tk_u0b, 2: None, 3: None}
        ln2_pend = None

        def ln2_stats(pend):
            tp, fV, fS = pend
            sy.waits("V", [fV, fS])
            for tb in range(4):
                base = tb * D4
                for kq in range(8):
                    vch(V.bn_stats, bnbuf[:, kq * 6:(kq + 1) * 6],
                        scr16[:, base + kq * 512: base + (kq + 1) * 512])
                vch(V.bn_aggr, bnagg4[:, 2 * tb:2 * tb + 2], bnbuf[:, :])
            va = AP(bnagg4[:, :].tensor, 1, [[8, 128], [2, 4]])
            return vch(V.tensor_scalar_add, vcol4[:, :], va, EPS)

        def ln2_scal(t_vc):
            sy.wait("S", t_vc)
            sch(S.activation, vcol4[:, :], vcol4[:, :], AF.Ln)
            return sch(S.activation, sigcol4[:, :], vcol4[:, :], AF.Exp, scale=-0.5)

        def ln2_norm(pend, t_sc):
            tp = pend[0]
            sy.wait("V", t_sc)
            for tb in range(4):
                base = tb * D4
                for hf in range(4):
                    op = outf[hf % 2]
                    sy.waits("V", [bank_w.get(("outf", hf % 2))])
                    t_of = vch(V.tensor_scalar, op[:, :],
                               scr16[:, base + hf * 1024: base + (hf + 1) * 1024],
                               bnagg4[:, 2 * tb:2 * tb + 1], sigcol4[:, tb:tb + 1],
                               OP.subtract, op1=OP.mult)
                    sy.wait("Y", t_of)
                    bank_w[("outf", hf % 2)] = ydma(f"YO{hf % 2}",
                                                    AP(out_d.ap().tensor,
                                                       (tp * TT + tb * 128) * D4 + hf * 1024,
                                                       [[D4, 128], [1, 1024]]), op[:, :])
            bank_r["scr16"] = sy.tok("V")

        for t in range(ntile):
            def w_dma(os):
                par = os % 2
                sy.waits("G", [bank_w.get(("wtb", par)), bank_w.get(("wexp", par)),
                               bank_w.get(("wadd", par))] + t_postsetup)
                if t == 0:
                    tk = gdma(f"GW{par}", wtb[par][:, :],
                              AP(WT.ap().tensor, os * 512,
                                 [[D4, 128], [128 * D4, NCH], [1, 512]]))
                    sy.waits("V", [tk, tk_eye])
                    for jjj in range(4):
                        it = os * 4 + jjj
                        dsl = slice(it * 512 + jjj * 128, it * 512 + (jjj + 1) * 128)
                        tk = vch(V.tensor_add, wtb[par][:, dsl], wtb[par][:, dsl], eyeb[:, :])
                    bank_w[("wadd", par)] = tk
                    bank_w[("wdma", os)] = tk
                else:
                    bank_w[("wdma", os)] = gdma(f"GW{par}", wtb[par][:, :], wscr.ap()[os, :])

            t_vc2 = ln2_stats(ln2_pend) if ln2_pend else None

            sy.waits("T", t_ssm + [bank_r.get("carrynxt0")])
            stat_dep = [None] * 16
            stats_fin = [None, None]

            def st_mm(c_):
                evA, sqA, evB, sqB = stat_dep[c_]
                first, last = (c_ == 0), (c_ == 15)
                deps = [evA, evB, sqA, sqB]
                if first:
                    deps += [bank_r.get(6), bank_r.get(7)]
                sy.waits("T", deps)
                for k2 in range(2):
                    dsl = slice((2 * c_ + k2) * TT, (2 * c_ + k2 + 1) * TT)
                    ssl = slice(((2 * c_ + k2) % 16) * TT, ((2 * c_ + k2) % 16 + 1) * TT)
                    mm1 = T.matmul(ps[6][0:1, 0:512], onescb[:, :], r1b[:, dsl],
                                   start=(first and k2 == 0), stop=(last and k2 == 1))
                    mm2 = T.matmul(ps[7][0:1, 0:512], onescb[:, :], uTb[:, ssl],
                                   start=(first and k2 == 0), stop=(last and k2 == 1))
                    if last and k2 == 1:
                        stats_fin[0] = sy.inc("T", mm1)
                        stats_fin[1] = sy.inc("T", mm2)
                    elif k2 == 1:
                        stats_fin[0] = sy.inc("T", mm2)

            for c in range(16):
                cur, nxt = c % 2, (c + 1) % 2
                biA, biB = c % 2, 2 + c % 2
                slA = slice(((2 * c) % 16) * TT, ((2 * c) % 16 + 1) * TT)
                slB = slice(((2 * c + 1) % 16) * TT, ((2 * c + 1) % 16 + 1) * TT)
                rsA = slice((2 * c) * TT, (2 * c + 1) * TT)
                rsB = slice((2 * c + 1) * TT, (2 * c + 2) * TT)
                sy.waits("T", [uq_tok[c // 4]])
                sy.waits("T", [bank_r.get("psC"), bank_r.get(4)])
                T.matmul(ps[4][0:128, 0:512], pstat[:, :], carry[cur][:, :], start=True, stop=False)
                T.matmul(ps[4][0:128, 0:512], win_sb[:, 0:128], uTb[:, slA], start=False, stop=False)
                cmt = sy.inc("T", T.matmul(ps[4][0:128, 0:512], win_sb[:, 128:256], uTb[:, slB], start=False, stop=True))
                bank_w["psC"] = cmt
                sy.waits("T", [bank_r.get(biA)])
                T.matmul(ps[biA][0:128, 0:512], band0b[:, :], uTb[:, slA], start=True, stop=False)
                amt = sy.inc("T", T.matmul(ps[biA][0:128, 0:512], wout_sb[:, 0:128], carry[cur][:, :], start=False, stop=True))
                bank_w[biA] = amt
                sy.waits("T", [bank_r.get(biB)])
                T.matmul(ps[biB][0:128, 0:512], band1b[:, :], uTb[:, slA], start=True, stop=False)
                T.matmul(ps[biB][0:128, 0:512], band0b[:, :], uTb[:, slB], start=False, stop=False)
                bmt = sy.inc("T", T.matmul(ps[biB][0:128, 0:512], wout_sb[:, 128:256], carry[cur][:, :], start=False, stop=True))
                bank_w[biB] = bmt
                if c > 0:
                    st_mm(c - 1)
                sy.waits("S", [cmt, bank_r.get("carrynxt%d" % nxt)])
                cv = sch(S.copy, carry[nxt][:, :], ps[4][0:128, 0:512])
                bank_r["psC"] = cv
                bank_r[4] = cv
                bank_r["carrynxt%d" % nxt] = cv
                sy.waits("S", [amt, bmt])
                evA = sch(S.copy, r1b[:, rsA], ps[biA][0:128, 0:512])
                sqA = sch(S.square, uTb[:, slA], r1b[:, rsA])
                bank_r[biA] = evA
                evB = sch(S.copy, r1b[:, rsB], ps[biB][0:128, 0:512])
                sqB = sch(S.square, uTb[:, slB], r1b[:, rsB])
                bank_r[biB] = evB
                stat_dep[c] = (evA, sqA, evB, sqB)
                if c == 4:
                    uq_tok[2] = u_dma(t, 2, [sy.tok("T"), sy.tok("S"), sy.tok("V")])
                if c == 8:
                    uq_tok[3] = u_dma(t, 3, [sy.tok("T"), sy.tok("S"), sy.tok("V")])
                if c == 11:
                    w_dma(0)
                    w_dma(1)
                if c == 12 and ln2_pend is not None:
                    t_sc2 = ln2_scal(t_vc2)
            st_mm(15)
            t_s1 = stats_fin[0]
            t_s2 = stats_fin[1]
            t_conv = sy.tok("V")
            t_convT = sy.tok("T")
            t_convS = sy.tok("S")
            sy.waits("V", [t_convS, t_convT])
            zc = vch(V.memset, carry[0][:, :], 0.0)
            bank_r["carrynxt0"] = zc

            sy.waits("V", [t_s1, t_s2])
            t_mr = vch(V.tensor_scalar_mul, m_row[0:1, :], ps[6][0:1, 0:512], 1.0 / D4)
            vch(V.tensor_mul, lnt2, m_row[0:1, :], m_row[0:1, :])
            e_st = vch(V.scalar_tensor_tensor, lnt2, ps[7][0:1, 0:512], 1.0 / D4, lnt2,
                       OP.mult, OP.subtract)
            bank_r[6] = e_st
            bank_r[7] = e_st
            t_var = vch(V.tensor_scalar_add, lnt2, lnt2, EPS)
            sy.wait("S", t_var)
            sch(S.activation, lnt2, lnt2, AF.Ln)
            t_sig = sch(S.activation, sigrow, lnt2, AF.Exp, scale=-0.5)
            sy.waits("T", [bank_r.get(2), tk_cat])
            for it in range(NCH):
                T.matmul(ps[2][0:36, 0:512], catb[:, it * 36:(it + 1) * 36],
                         r1b[:, it * TT:(it + 1) * TT], start=(it == 0), stop=False)
            sy.waits("T", [t_mr])
            t_pj = sy.inc("T", T.matmul(ps[2][0:36, 0:512], caneg[0:1, :], m_row[0:1, :],
                                        start=False, stop=True))
            sy.waits("T", [bank_r.get(0), bank_r.get(1), sy.tok("V"), t_sig])
            m1 = sy.inc("T", T.matmul(ps[0][0:128, 0:512], onesr[0:1, :], m_row[0:1, :], start=True, stop=True))
            m2 = sy.inc("T", T.matmul(ps[1][0:128, 0:512], onesr[0:1, :], sigrow, start=True, stop=True))
            bank_w[0], bank_w[1] = m1, m2
            sy.waits("S", [m1, m2])
            sch(S.copy, mB[:, :], ps[0][0:128, 0:512])
            t_sigB = sch(S.copy, sigB[:, :], ps[1][0:128, 0:512])
            bank_r[0] = t_sigB
            bank_r[1] = t_sigB
            sy.waits("V", [t_sigB, t_s1, t_s2, t_convS])
            mB4 = AP(mB[:, :].tensor, 0, [[512, 128], [0, 4], [1, 512]])
            sigB4 = AP(sigB[:, :].tensor, 0, [[512, 128], [0, 4], [1, 512]])
            sy.waits("T", [bank_r[0], tk_cat, bank_w.get(("wdma", 0))])
            e_pj = t_gv = None
            for cq in range(8):
                cs = slice(cq * 2048, (cq + 1) * 2048)
                vch(V.tensor_sub, nt[:, :], r1b[:, cs], mB4)
                tnq = vch(V.tensor_mul, r1b[:, cs], nt[:, :], sigB4)
                if cq == 3:
                    sy.wait("V", t_pj)
                    e_pj = vch(V.tensor_copy, projf, ps[2][0:32, 0:512])
                    t_gv = vch(V.tensor_mul, geexp, ps[2][32:36, 0:512], sigB[0:4, :])
                    bank_r[2] = t_gv
                    sy.waits("S", [t_gv])
                    t_ge = sch(S.activation, geexp, geexp, AF.Exp)
                sy.waits("T", [tnq])
                for it in range(4 * cq, 4 * cq + 4):
                    T.matmul(ps[0][0:128, 0:512], r1b[:, it * TT: it * TT + 128],
                             wtb[0][:, it * 512:(it + 1) * 512], start=(it == 0), stop=False)
                if cq == 3:
                    sy.waits("T", [t_ge, tk_oc, bank_r.get(7)])
                    t_gs = sy.inc("T", T.matmul(ps[7][0:1, 0:512], onesc[0:4, 0:1], geexp, start=True, stop=True))
                    bank_w[7] = t_gs
                    sy.wait("S", t_gs)
                    sch(S.activation, ginv2, ps[7][0:1, 0:512], AF.Ln)
                    t_gi = sch(S.activation, ginv2, ginv2, AF.Exp, scale=-1.0)
                    bank_r[7] = t_gi
                    sy.waits("T", [bank_r[7], t_gi])
                    t_b4 = sy.inc("T", T.matmul(ps[7][0:4, 0:512], onesr[0:1, 0:4], ginv2, start=True, stop=True))
                    bank_w[7] = t_b4
            t_x1 = sy.tok("V")
            sy.waits("V", [t_b4, t_ge])
            t_g4 = vch(V.tensor_mul, gates4, geexp, ps[7][0:4, 0:512])
            t_g4s = vch(V.tensor_mul, gates4, gates4, sigB[0:4, :])
            bank_r[7] = t_g4s
            sy.waits("T", [bank_r[7], t_g4s, tk_e4])
            t_ex = sy.inc("T", T.matmul(ps[7][0:32, 0:512], exp4sb[:, :], gates4, start=True, stop=True))
            bank_w[7] = t_ex
            sy.waits("V", [t_ex, e_pj])
            t_sp = vch(V.tensor_mul, sprojb, projf, ps[7][0:32, 0:512])
            bank_r[7] = t_sp
            if ln2_pend is not None:
                ln2_norm(ln2_pend, t_sc2)
                ln2_pend = None

            for os in range(8):
                par = os % 2
                tk_w = bank_w.pop(("wdma", os), None)
                for tb in range(4):
                    bi = (os * 4 + tb) % 8
                    sy.waits("T", [bank_r.get(bi), tk_w, t_x1, tk_setup, tk_bmt])
                    if not (os == 0 and tb == 0):
                        for it in range(NCH):
                            T.matmul(ps[bi][0:128, 0:512],
                                     r1b[:, it * TT + tb * 128: it * TT + (tb + 1) * 128],
                                     wtb[par][:, it * 512:(it + 1) * 512],
                                     start=(it == 0), stop=False)
                    sy.waits("T", [t_sp, t_onerow])
                    lm = T.matmul(ps[bi][0:128, 0:512],
                                  sproj2[0:33, tb * 128:(tb + 1) * 128],
                                  bmtb[0:33, os * 512:(os + 1) * 512], start=False, stop=True)
                    bank_w[bi] = sy.inc("T", lm)
                    dst = scr16[:, tb * D4 + os * 512: tb * D4 + (os + 1) * 512]
                    if (os + tb) % 2 == 0:
                        sy.waits("V", [bank_w[bi], bank_r.get("scr16")])
                        ev = vch(V.tensor_copy, dst, ps[bi][0:128, 0:512])
                    else:
                        sy.waits("S", [bank_w[bi], bank_r.get("scr16")])
                        ev = sch(S.copy, dst, ps[bi][0:128, 0:512])
                    bank_r[bi] = ev
                bank_w[("wtb", par)] = sy.tok("T")
                if t == 0:
                    sy.waits("Y", [tk_w])
                    bank_w[("wexp", par)] = ydma(f"YW{par}", wscr.ap()[os, :], wtb[par][:, :])
                if os + 2 < 8:
                    w_dma(os + 2)
                if os == 4 and t + 1 < ntile:
                    uq_tok[0] = u_dma(t + 1, 0, [t_conv, t_convT, t_convS])
                if os == 5 and t + 1 < ntile:
                    uq_tok[1] = u_dma(t + 1, 1, [])
            ln2_pend = (t, sy.tok("V"), sy.tok("S"))

        t_vc2 = ln2_stats(ln2_pend)
        t_sc2 = ln2_scal(t_vc2)
        ln2_norm(ln2_pend, t_sc2)

        for k, semh in sy.sem.items():
            if sy.cnt[k]:
                eng = sy.eng["Y"] if k[0] in "YG" else sy.eng[k]
                eng.wait_ge(semh, sy.cnt[k])
    return nc


def _host_prep(inputs, ntile):
    ntok = ntile * TT
    f32 = np.float32
    u = np.ascontiguousarray(inputs["L0_attn_u"], dtype=f32).reshape(-1, D4)
    WTh = np.ascontiguousarray(np.asarray(inputs["L0_ffn_W"], dtype=f32).T)
    CatAh = np.ascontiguousarray(np.concatenate([
        np.asarray(inputs["L0_ffn_A"], dtype=f32).transpose(2, 0, 1).reshape(D4, 32),
        np.asarray(inputs["L0_ffn_Wr"], dtype=f32).T], axis=1))
    BmTh = np.ascontiguousarray(np.concatenate([
        np.asarray(inputs["L0_ffn_Bm"], dtype=f32).transpose(0, 2, 1).reshape(32, D4),
        np.asarray(inputs["L0_ffn_b"], dtype=f32).reshape(1, D4)], axis=0))
    biash = np.ascontiguousarray(np.asarray(inputs["L0_ffn_b"], dtype=f32))
    lam = np.asarray(inputs["L0_attn_Lambda_ri"], dtype=f32)
    pv = np.asarray(inputs["L0_attn_p_ri"], dtype=f32)
    qv = np.asarray(inputs["L0_attn_q_ri"], dtype=f32)
    ctv = np.asarray(inputs["L0_attn_Ct_ri"], dtype=f32).reshape(64, 2)
    vc = np.asarray(inputs["L0_attn_Vc_ri"], dtype=f32)
    smalls = {
        "lam_re": lam[:, 0], "lam_im": lam[:, 1],
        "p_re": pv[:, 0], "p_im": pv[:, 1],
        "q_re": qv[:, 0], "q_im": qv[:, 1],
        "ct_re": ctv[:, 0], "ct_im": ctv[:, 1],
        "Bpv": np.asarray(inputs["L0_attn_Bp"], dtype=f32).reshape(64),
    }
    smalls = {k: np.ascontiguousarray(np.concatenate([v, v])) for k, v in smalls.items()}
    VcT_re = np.ascontiguousarray(vc[..., 0].T)
    VcT_im = np.ascontiguousarray(vc[..., 1].T)
    Dh = np.ascontiguousarray(np.asarray(inputs["L0_attn_D"], dtype=f32).reshape(1))
    lsh = np.ascontiguousarray(np.asarray(inputs["L0_attn_log_step"], dtype=f32).reshape(1))

    k_ = np.arange(D4)
    Om = np.exp(-2j * np.pi * (k_ / D4))
    cval = (1.0 - Om) / (1.0 + Om)
    cc = 2.0 / (1.0 + Om)
    cvr = cval.real.astype(f32).reshape(2, 2048)
    cvi = cval.imag.astype(f32).reshape(2, 2048)
    cvb_re = np.ascontiguousarray(np.repeat(cvr, 64, axis=0))
    cvb_im = np.ascontiguousarray(np.repeat(cvi, 64, axis=0))
    cc_re = np.ascontiguousarray(cc.real.astype(f32).reshape(128, 32))
    cc_im = np.ascontiguousarray(cc.imag.astype(f32).reshape(128, 32))
    F = np.exp(2j * np.pi * np.outer(np.arange(64), np.arange(64)) / 64.0)
    Tw = np.exp(2j * np.pi * np.outer(np.arange(64), np.arange(64)) / D4)
    ftab = np.ascontiguousarray(np.concatenate([
        F.real, F.imag, -F.imag, F.real / D4, -F.imag / D4, Tw.real, Tw.imag,
    ], axis=1).astype(f32))
    J128 = np.ascontiguousarray(np.eye(128, dtype=f32)[::-1])
    eye128 = np.eye(128, dtype=f32)
    ones_row = np.ones((1, 128), dtype=f32)
    ones_col = np.ones((128, 1), dtype=f32)
    EXP4 = np.zeros((4, 32), dtype=f32)
    for e in range(4):
        EXP4[e, e * 8:(e + 1) * 8] = 2.0

    common = dict(WT=WTh, CatA=CatAh, BmTd=BmTh, biasd=biash,
                  VcT_re=VcT_re, VcT_im=VcT_im, Dv=Dh, lsd=lsh,
                  cvb_re=cvb_re, cvb_im=cvb_im, cc_re=cc_re, cc_im=cc_im,
                  ftab=ftab, J128=J128, eye128=eye128,
                  ones_row=ones_row, ones_col=ones_col, EXP4=EXP4, **smalls)
    in_maps = []
    for c in range(NCORES):
        m = dict(common)
        m["uT"] = np.ascontiguousarray(u[c * ntok:(c + 1) * ntok].T)
        in_maps.append(m)
    return in_maps


def kernel(**inputs):
    ntile = 2048 // TT
    in_maps = _host_prep(inputs, ntile)
    nc = build_nc(ntile)
    res = run_bass_kernel_spmd(nc, in_maps, core_ids=list(range(NCORES)))
    out = np.concatenate([res.results[i]["o"] for i in range(NCORES)], axis=0)
    return out.reshape(2, 8192, D4)
